# version 01
# baseline (speedup 1.0000x reference)
import numpy as np
import ml_dtypes
from contextlib import ExitStack

import concourse.bass as bass
import concourse.mybir as mybir
from concourse.bass_utils import run_bass_kernel_spmd

F32 = mybir.dt.float32
BF16 = mybir.dt.bfloat16
AF = mybir.ActivationFunctionType
ALU = mybir.AluOpType
ts = bass.ts

P = 128
D = 1024
DH = 4096
NH, DK = 16, 64
S = 4096
T = 512
NT = S // T
KC = D // P
HC = DH // P
N_CORES = 8
EPS = 1e-5


def build_program():
    nc = bass.Bass("TRN2", target_bir_lowering=False)

    xT = nc.dram_tensor("xT", [D, S], F32, kind="ExternalInput").ap()
    wqkv = nc.dram_tensor("wqkv", [D, 3 * D], F32, kind="ExternalInput").ap()
    bqkv = nc.dram_tensor("bqkv", [P, 3 * D // P], F32, kind="ExternalInput").ap()
    wout = nc.dram_tensor("wout", [D, D], F32, kind="ExternalInput").ap()
    bout = nc.dram_tensor("bout", [P, D // P], F32, kind="ExternalInput").ap()
    w1T = nc.dram_tensor("w1T", [D, DH], F32, kind="ExternalInput").ap()
    w2T = nc.dram_tensor("w2T", [D, DH], F32, kind="ExternalInput").ap()
    h1T = nc.dram_tensor("h1T", [DH, DH], F32, kind="ExternalInput").ap()
    h2T = nc.dram_tensor("h2T", [DH, DH], F32, kind="ExternalInput").ap()
    w3T = nc.dram_tensor("w3T", [DH, D], F32, kind="ExternalInput").ap()
    s1 = nc.dram_tensor("s1", [P, HC], F32, kind="ExternalInput").ap()
    s2 = nc.dram_tensor("s2", [P, HC], F32, kind="ExternalInput").ap()
    sh1 = nc.dram_tensor("sh1", [P, HC], F32, kind="ExternalInput").ap()
    sh2 = nc.dram_tensor("sh2", [P, HC], F32, kind="ExternalInput").ap()
    s3 = nc.dram_tensor("s3", [P, D // P], F32, kind="ExternalInput").ap()
    c_bo = nc.dram_tensor("c_bo", [P, P], BF16, kind="ExternalInput").ap()
    c_dsel = nc.dram_tensor("c_dsel", [P, DK], BF16, kind="ExternalInput").ap()
    c_dn = nc.dram_tensor("c_dn", [P, 1], BF16, kind="ExternalInput").ap()
    c_ones = nc.dram_tensor("c_ones", [P, 1], BF16, kind="ExternalInput").ap()
    c_row = nc.dram_tensor("c_row", [1, P], BF16, kind="ExternalInput").ap()
    c_row64 = nc.dram_tensor("c_row64", [P, P], BF16, kind="ExternalInput").ap()
    c_eps = nc.dram_tensor("c_eps", [P, 1], F32, kind="ExternalInput").ap()

    outT = nc.dram_tensor("outT", [D, S], F32, kind="ExternalOutput").ap()

    x1_d = nc.dram_tensor("x1_d", [D, S], F32, kind="Internal").ap()
    x1ln_d = nc.dram_tensor("x1ln_d", [D, S], BF16, kind="Internal").ap()
    wb1_d = nc.dram_tensor("wb1_d", [D, DH], BF16, kind="Internal").ap()
    wb2_d = nc.dram_tensor("wb2_d", [D, DH], BF16, kind="Internal").ap()
    hb1_d = nc.dram_tensor("hb1_d", [DH, DH], BF16, kind="Internal").ap()
    hb2_d = nc.dram_tensor("hb2_d", [DH, DH], BF16, kind="Internal").ap()
    wb3_d = nc.dram_tensor("wb3_d", [DH, D], BF16, kind="Internal").ap()

    with ExitStack() as g:
        E = g.enter_context
        bo_s = E(nc.sbuf_tensor("bo_s", [P, P], BF16))
        dsel_s = E(nc.sbuf_tensor("dsel_s", [P, DK], BF16))
        dn_s = E(nc.sbuf_tensor("dn_s", [P, 1], BF16))
        ones_s = E(nc.sbuf_tensor("ones_s", [P, 1], BF16))
        row_s = E(nc.sbuf_tensor("row_s", [1, P], BF16))
        row64_s = E(nc.sbuf_tensor("row64_s", [P, P], BF16))
        eps_s = E(nc.sbuf_tensor("eps_s", [P, 1], F32))
        bqkv_s = E(nc.sbuf_tensor("bqkv_s", [P, 3 * D // P], F32))
        bout_s = E(nc.sbuf_tensor("bout_s", [P, D // P], F32))
        s1_s = E(nc.sbuf_tensor("s1_s", [P, HC], F32))
        s2_s = E(nc.sbuf_tensor("s2_s", [P, HC], F32))
        sh1_s = E(nc.sbuf_tensor("sh1_s", [P, HC], F32))
        sh2_s = E(nc.sbuf_tensor("sh2_s", [P, HC], F32))
        s3_s = E(nc.sbuf_tensor("s3_s", [P, D // P], F32))

        pb = [E(nc.psum_tensor(f"pb{i}", [P, T], F32)) for i in range(8)]

        SW = E(nc.semaphore("SW"))
        SI = E(nc.semaphore("SI"))
        SO = E(nc.semaphore("SO"))
        STn = E(nc.semaphore("STn"))
        SV = E(nc.semaphore("SV"))
        SS = E(nc.semaphore("SS"))
        blk = E(nc.Block())
        C = dict(w=0, i=0, o=0, t=0, v=0, s=0)
        SEMS = {"t": STn, "v": SV, "s": SS, "i": SI, "o": SO, "w": SW}

        def t_inc(instr):
            C["t"] += 1
            return instr.then_inc(STn, 1)

        def v_inc(instr):
            C["v"] += 1
            return instr.then_inc(SV, 1)

        def s_inc(instr):
            C["s"] += 1
            return instr.then_inc(SS, 1)

        def wt(e, tok):
            if tok is None:
                return
            if tok[0] == "multi":
                for sub in tok[1]:
                    wt(e, sub)
                return
            if tok[1] > 0:
                e.wait_ge(SEMS[tok[0]], tok[1])

        @blk.sync
        def _(e):
            for dst, src in [(bo_s, c_bo), (dsel_s, c_dsel), (dn_s, c_dn),
                             (ones_s, c_ones), (row_s, c_row), (row64_s, c_row64), (eps_s, c_eps),
                             (bqkv_s, bqkv), (bout_s, bout),
                             (s1_s, s1), (s2_s, s2), (sh1_s, sh1), (sh2_s, sh2), (s3_s, s3)]:
                e.dma_start(out=dst[:], in_=src).then_inc(SI, 16)
                C["i"] += 16
        consts_loaded = C["i"]

        @blk.tensor
        def _(e):
            e.wait_ge(SI, consts_loaded)
        @blk.vector
        def _(e):
            e.wait_ge(SI, consts_loaded)
        @blk.scalar
        def _(e):
            e.wait_ge(SI, consts_loaded)

        with ExitStack() as pa:
            EA = pa.enter_context
            wqkv_s = EA(nc.sbuf_tensor("wqkv_s", [P, KC * 3 * D], BF16))
            wout_s = EA(nc.sbuf_tensor("wout_s", [P, KC * D], BF16))
            xbuf = [EA(nc.sbuf_tensor(f"xbuf{j}", [P, KC * T], F32)) for j in range(2)]
            xbb = [EA(nc.sbuf_tensor(f"xbb{j}", [P, T], BF16)) for j in range(2)]
            sqb = [EA(nc.sbuf_tensor(f"sqb{j}", [P, T], BF16)) for j in range(2)]
            t1f = EA(nc.sbuf_tensor("t1f", [P, T], F32))
            rows = EA(nc.sbuf_tensor("rows", [1, 5 * T], F32))
            rowb = EA(nc.sbuf_tensor("rowb", [1, 2 * T], BF16))
            xln_s = EA(nc.sbuf_tensor("xln_s", [P, KC * T], BF16))
            q_s = EA(nc.sbuf_tensor("q_s", [P, KC * T], BF16))
            k_s = EA(nc.sbuf_tensor("k_s", [P, KC * T], BF16))
            v_s = EA(nc.sbuf_tensor("v_s", [P, KC * T], BF16))
            qrep = EA(nc.sbuf_tensor("qrep", [P, T], BF16))
            tmp = [EA(nc.sbuf_tensor(f"tmp{j}", [P, T], BF16)) for j in range(2)]
            esb = [EA(nc.sbuf_tensor(f"esb{j}", [P, T], BF16)) for j in range(3)]
            tmp2 = [EA(nc.sbuf_tensor(f"tmp2_{j}", [P, T], BF16)) for j in range(2)]
            rband = EA(nc.sbuf_tensor("rband", [P, T], F32))
            rbandb = EA(nc.sbuf_tensor("rbandb", [P, 2 * T], BF16))
            ctxu = EA(nc.sbuf_tensor("ctxu", [P, KC * T], BF16))
            ctxb = EA(nc.sbuf_tensor("ctxb", [P, KC * T], BF16))
            x1_s = EA(nc.sbuf_tensor("x1_s", [P, KC * T], F32))
            x1ln_s = EA(nc.sbuf_tensor("x1ln_s", [P, KC * T], BF16))

            @blk.gpsimd
            def _(e):
                e.dma_start(out=wqkv_s[:].rearrange("p (k m) -> p k m", k=KC),
                            in_=wqkv.rearrange("(k p) m -> p k m", p=P)).then_inc(SW, 16)
                e.dma_start(out=wout_s[:].rearrange("p (k m) -> p k m", k=KC),
                            in_=wout.rearrange("(k p) m -> p k m", p=P)).then_inc(SW, 16)
            C["w"] += 32
            qkvw_loaded = C["w"]

            conv = {}
            @blk.gpsimd
            def _(e):
                for name, src, dst, nrows in [("w1", w1T, wb1_d, D), ("w2", w2T, wb2_d, D),
                                              ("h1", h1T, hb1_d, DH), ("h2", h2T, hb2_d, DH),
                                              ("w3", w3T, wb3_d, DH)]:
                    step = 256
                    for r0 in range(0, nrows, step):
                        e.dma_start(out=dst[r0:r0 + step, :], in_=src[r0:r0 + step, :]).then_inc(SW, 16)
                        C["w"] += 16
                    conv[name] = C["w"]

            x_free = [None, None]
            x_loaded = [0, 0]
            x1_store = None
            x1ln_store = None
            rowb_free = None
            xln_free = None
            qkv_free = None
            ctxu_free = None
            ctxb_free = None
            pbank_free = [None] * 8

            @blk.sync
            def _(e):
                e.dma_start(out=xbuf[0][:].rearrange("p (k t) -> p k t", k=KC),
                            in_=xT[:, 0:T].rearrange("(k p) t -> p k t", p=P)).then_inc(SI, 16)
            C["i"] += 16
            x_loaded[0] = C["i"]

            def emit_ln(src_f32, dst_bf):
                nonlocal rowb_free
                stat_bank, b1, b2 = 4, 2, 3
                sq_ready = [0, 0]
                for c in range(KC):
                    @blk.vector
                    def _(e, c=c):
                        e.tensor_copy(out=xbb[c % 2][:], in_=src_f32[:, ts(c, T)])
                        v_inc(e.tensor_tensor(out=sqb[c % 2][:], in0=xbb[c % 2][:], in1=xbb[c % 2][:], op=ALU.mult))
                    sq_ready[c % 2] = C["v"]

                    @blk.tensor
                    def _(e, c=c):
                        e.wait_ge(SV, sq_ready[c % 2])
                        if c == 0:
                            wt(e, pbank_free[stat_bank])
                        e.matmul(out=pb[stat_bank][0:1, :], lhsT=ones_s[:], rhs=xbb[c % 2][:],
                                 start=(c == 0), stop=(c == KC - 1), skip_group_check=True)
                        t_inc(e.matmul(out=pb[stat_bank][32:33, :], lhsT=ones_s[:], rhs=sqb[c % 2][:],
                                       start=(c == 0), stop=(c == KC - 1), skip_group_check=True,
                                       tile_position=(0, 32)))
                stats_done = C["t"]

                m_row = rows[:, 0:T]; v_row = rows[:, T:2 * T]; msq_row = rows[:, 2 * T:3 * T]
                r_row = rows[:, 3 * T:4 * T]; mr_row = rows[:, 4 * T:5 * T]

                @blk.vector
                def _(e):
                    e.wait_ge(STn, stats_done)
                    e.tensor_scalar(out=m_row, in0=pb[stat_bank][0:1, :], scalar1=1.0 / D, scalar2=None, op0=ALU.mult)
                    e.tensor_scalar(out=v_row, in0=pb[stat_bank][32:33, :], scalar1=1.0 / D, scalar2=None, op0=ALU.mult)
                    e.tensor_tensor(out=msq_row, in0=m_row, in1=m_row, op=ALU.mult)
                    v_inc(e.tensor_tensor(out=v_row, in0=v_row, in1=msq_row, op=ALU.subtract))
                var_done = C["v"]
                pbank_free[stat_bank] = ("v", var_done)

                @blk.scalar
                def _(e):
                    e.wait_ge(SV, var_done)
                    e.activation(out=r_row, in_=v_row, func=AF.Ln, bias=eps_s[0:1, :])
                    s_inc(e.activation(out=r_row, in_=r_row, func=AF.Exp, scale=-0.5))
                r_done = C["s"]

                @blk.vector
                def _(e):
                    e.wait_ge(SS, r_done)
                    wt(e, rowb_free)
                    e.tensor_tensor(out=mr_row, in0=m_row, in1=r_row, op=ALU.mult)
                    e.tensor_copy(out=rowb[:, 0:T], in_=r_row)
                    v_inc(e.tensor_copy(out=rowb[:, T:2 * T], in_=mr_row))
                rowb_done = C["v"]

                @blk.tensor
                def _(e):
                    e.wait_ge(SV, rowb_done)
                    wt(e, pbank_free[b1])
                    wt(e, pbank_free[b2])
                    e.matmul(out=pb[b1][:], lhsT=row_s[:], rhs=rowb[:, 0:T], start=True, stop=True)
                    t_inc(e.matmul(out=pb[b2][:], lhsT=row_s[:], rhs=rowb[:, T:2 * T], start=True, stop=True))
                bcast_done = C["t"]
                rowb_free = ("t", bcast_done)

                for c in range(KC):
                    @blk.vector
                    def _(e, c=c):
                        if c == 0:
                            e.wait_ge(STn, bcast_done)
                        e.tensor_tensor(out=t1f[:], in0=src_f32[:, ts(c, T)], in1=pb[b1][:], op=ALU.mult)
                        v_inc(e.tensor_tensor(out=dst_bf[:, ts(c, T)], in0=t1f[:], in1=pb[b2][:], op=ALU.subtract))
                apply_done = C["v"]
                pbank_free[b1] = ("v", apply_done)
                pbank_free[b2] = ("v", apply_done)
                return apply_done

            for i in range(NT):
                xb_i = xbuf[i % 2]

                if i + 1 < NT:
                    @blk.sync
                    def _(e, i=i):
                        wt(e, x_free[(i + 1) % 2])
                        e.dma_start(out=xbuf[(i + 1) % 2][:].rearrange("p (k t) -> p k t", k=KC),
                                    in_=xT[:, ts(i + 1, T)].rearrange("(k p) t -> p k t", p=P)).then_inc(SI, 16)
                    C["i"] += 16
                    x_loaded[(i + 1) % 2] = C["i"]

                @blk.vector
                def _(e, i=i):
                    e.wait_ge(SI, x_loaded[i % 2])
                    wt(e, xln_free)
                    e.engine_nop()
                ln1_done = emit_ln(xb_i, xln_s)

                mm_done_at = {}
                for m in range(3 * D // P):
                    bank = m % 2
                    @blk.tensor
                    def _(e, m=m, bank=bank):
                        if m == 0:
                            e.wait_ge(SV, ln1_done)
                            e.wait_ge(SW, qkvw_loaded)
                        wt(e, pbank_free[bank])
                        for k in range(KC):
                            ins = e.matmul(out=pb[bank][:],
                                           lhsT=wqkv_s[:, k * (3 * D) + m * P: k * (3 * D) + (m + 1) * P],
                                           rhs=xln_s[:, ts(k, T)], start=(k == 0), stop=(k == KC - 1))
                            if k == KC - 1:
                                t_inc(ins)
                    mm_done_at[m] = C["t"]

                    dst = (q_s, k_s, v_s)[m // KC]
                    col = (m % KC) * T
                    @blk.scalar
                    def _(e, m=m, bank=bank, dst=dst, col=col):
                        e.wait_ge(STn, mm_done_at[m])
                        if m == 0:
                            wt(e, qkv_free)
                        s_inc(e.activation(out=dst[:, col:col + T], in_=pb[bank][:], func=AF.Identity,
                                           bias=bqkv_s[:, m:m + 1]))
                    pbank_free[bank] = ("s", C["s"])
                qkv_done = C["s"]
                xln_free = ("t", mm_done_at[3 * D // P - 1])

                tmp_free = [0, 0]
                tmp2_free = [0, 0]
                esb_free = [0, 0, 0]
                psU_free = [None, None]
                rband_free = 0
                rbandb_free = [0, 0]
                uevac_last = 0

                for h in range(NH):
                    c_q, r_q = h // 2, (h % 2) * DK
                    ub = 4 + (h % 2)

                    @blk.vector
                    def _(e, h=h, c_q=c_q, r_q=r_q):
                        if h == 0:
                            e.wait_ge(SS, qkv_done)
                        e.tensor_copy(out=qrep[0:DK, :], in_=q_s[r_q:r_q + DK, ts(c_q, T)])
                        v_inc(e.tensor_copy(out=qrep[DK:P, :], in_=q_s[r_q:r_q + DK, ts(c_q, T)]))
                    qrep_done = C["v"]

                    for c in range(KC):
                        eb_i = (h * KC + c) % 3
                        eb = 2 + (c % 2)

                        @blk.vector
                        def _(e, c=c):
                            if tmp_free[c % 2] > 0:
                                e.wait_ge(STn, tmp_free[c % 2])
                            v_inc(e.tensor_tensor(out=tmp[c % 2][:], in0=k_s[:, ts(c, T)], in1=qrep[:], op=ALU.mult))
                        tmp_done = C["v"]

                        @blk.tensor
                        def _(e, c=c, eb=eb):
                            e.wait_ge(SV, tmp_done)
                            wt(e, pbank_free[eb])
                            t_inc(e.matmul(out=pb[eb][:], lhsT=bo_s[:], rhs=tmp[c % 2][:], start=True, stop=True))
                        rep_done = C["t"]
                        tmp_free[c % 2] = rep_done

                        @blk.scalar
                        def _(e, c=c, eb=eb, eb_i=eb_i):
                            e.wait_ge(STn, rep_done)
                            if esb_free[eb_i] > 0:
                                e.wait_ge(STn, esb_free[eb_i])
                            s_inc(e.activation(out=esb[eb_i][:], in_=pb[eb][:], func=AF.Exp, scale=0.125))
                        exp_done = C["s"]
                        pbank_free[eb] = ("s", exp_done)

                        @blk.vector
                        def _(e, c=c, eb_i=eb_i):
                            e.wait_ge(SS, exp_done)
                            if tmp2_free[c % 2] > 0:
                                e.wait_ge(STn, tmp2_free[c % 2])
                            v_inc(e.tensor_tensor(out=tmp2[c % 2][:], in0=esb[eb_i][:], in1=v_s[:, ts(c, T)], op=ALU.mult))
                        t2_done = C["v"]

                        @blk.tensor
                        def _(e, c=c, h=h, ub=ub, eb_i=eb_i):
                            e.wait_ge(SV, t2_done)
                            if c == 0:
                                wt(e, psU_free[h % 2])
                            e.matmul(out=pb[ub][0:DK, :], lhsT=dsel_s[:], rhs=tmp2[c % 2][:],
                                     start=(c == 0), stop=(c == KC - 1), skip_group_check=True)
                            t_inc(e.matmul(out=pb[ub][DK:DK + 1, :], lhsT=dn_s[:], rhs=esb[eb_i][:],
                                           start=(c == 0), stop=(c == KC - 1), skip_group_check=True,
                                           tile_position=(0, DK)))
                        u_step = C["t"]
                        tmp2_free[c % 2] = u_step
                        esb_free[eb_i] = u_step
                    u_done = C["t"]

                    rb_slot = rbandb[64:65, ts(h % 2, T)]
                    @blk.scalar
                    def _(e, h=h, ub=ub, rb_slot=rb_slot, c_q=c_q, r_q=r_q):
                        e.wait_ge(STn, u_done)
                        if rbandb_free[h % 2] > 0:
                            e.wait_ge(STn, rbandb_free[h % 2])
                        e.activation(out=rband[64:65, :], in_=pb[ub][DK:DK + 1, :], func=AF.Ln)
                        s_inc(e.activation(out=rb_slot, in_=rband[64:65, :], func=AF.Exp, scale=-1.0))
                        if h < 2:
                            wt(e, ctxu_free)
                        s_inc(e.activation(out=ctxu[r_q:r_q + DK, ts(c_q, T)], in_=pb[ub][0:DK, :], func=AF.Copy))
                    r_and_evac = C["s"]
                    psU_free[h % 2] = ("s", r_and_evac)
                    uevac_last = r_and_evac

                    rb_bank = 6 + (h % 2)
                    @blk.tensor
                    def _(e, h=h, rb_bank=rb_bank, rb_slot=rb_slot):
                        e.wait_ge(SS, r_and_evac - 1)
                        wt(e, pbank_free[rb_bank])
                        t_inc(e.matmul(out=pb[rb_bank][:], lhsT=row64_s[64:65, :], rhs=rb_slot, start=True, stop=True))
                    rrep_done = C["t"]
                    rbandb_free[h % 2] = rrep_done

                    @blk.vector
                    def _(e, h=h, rb_bank=rb_bank, c_q=c_q, r_q=r_q):
                        e.wait_ge(STn, rrep_done)
                        e.wait_ge(SS, r_and_evac)
                        if h < 2:
                            wt(e, ctxb_free)
                        v_inc(e.tensor_tensor(out=ctxb[r_q:r_q + DK, ts(c_q, T)],
                                              in0=ctxu[r_q:r_q + DK, ts(c_q, T)],
                                              in1=pb[rb_bank][r_q:r_q + DK, :], op=ALU.mult))
                    pbank_free[rb_bank] = ("v", C["v"])
                ctx_done = C["v"]
                ctxu_free = ("v", ctx_done)
                qkv_free = ("v", ctx_done)
                pbank_free[4] = psU_free[0]
                pbank_free[5] = psU_free[1]

                x1_parts = 0
                for m in range(KC):
                    bank = m % 2
                    @blk.tensor
                    def _(e, m=m, bank=bank):
                        if m == 0:
                            e.wait_ge(SV, ctx_done)
                        wt(e, pbank_free[bank])
                        for k in range(KC):
                            ins = e.matmul(out=pb[bank][:],
                                           lhsT=wout_s[:, k * D + m * P: k * D + (m + 1) * P],
                                           rhs=ctxb[:, ts(k, T)], start=(k == 0), stop=(k == KC - 1))
                            if k == KC - 1:
                                t_inc(ins)
                    wmm = C["t"]
                    if m == KC - 1:
                        ctxb_free = ("t", wmm)

                    @blk.vector
                    def _(e, m=m, bank=bank, wmm=wmm):
                        e.wait_ge(STn, wmm)
                        if m == 0:
                            wt(e, x1_store)
                        v_inc(e.scalar_tensor_tensor(out=x1_s[:, ts(m, T)], in0=pb[bank][:],
                                                     scalar=bout_s[:, m:m + 1], in1=xb_i[:, ts(m, T)],
                                                     op0=ALU.add, op1=ALU.add))
                    pbank_free[bank] = ("v", C["v"])
                x1_done = C["v"]
                x_free[i % 2] = ("v", x1_done)

                @blk.sync
                def _(e, i=i):
                    e.wait_ge(SV, x1_done)
                    e.dma_start(out=x1_d[:, ts(i, T)].rearrange("(k p) t -> p k t", p=P),
                                in_=x1_s[:].rearrange("p (k t) -> p k t", k=KC)).then_inc(SO, 16)
                C["o"] += 16
                x1_store = ("o", C["o"])

                @blk.vector
                def _(e):
                    wt(e, x1ln_store)
                    e.engine_nop()
                ln2_done = emit_ln(x1_s, x1ln_s)

                @blk.sync
                def _(e, i=i):
                    e.wait_ge(SV, ln2_done)
                    e.dma_start(out=x1ln_d[:, ts(i, T)].rearrange("(k p) t -> p k t", p=P),
                                in_=x1ln_s[:].rearrange("p (k t) -> p k t", k=KC)).then_inc(SO, 16)
                C["o"] += 16
                x1ln_store = ("o", C["o"])

            phaseA_stores = C["o"]

        with ExitStack() as pbx:
            EB = pbx.enter_context
            wsl = [EB(nc.sbuf_tensor(f"wsl{j}", [P, HC * T], BF16)) for j in range(2)]
            x1lnB = [EB(nc.sbuf_tensor(f"x1lnB{j}", [P, KC * T], BF16)) for j in range(2)]
            hB = EB(nc.sbuf_tensor("hB", [P, HC * T], BF16))
            g1B = EB(nc.sbuf_tensor("g1B", [P, HC * T], BF16))
            sA = [EB(nc.sbuf_tensor(f"sA{j}", [P, T], BF16)) for j in range(2)]
            x1B = EB(nc.sbuf_tensor("x1B", [P, KC * T], F32))
            outB = EB(nc.sbuf_tensor("outB", [P, KC * T], F32))

            slot_free = [None, None]
            x1ln_loadedB = [0, 0]
            x1ln_freeB = [None, None]
            x1_loadedB = 0
            x1_usedB = 0
            out_stored = None
            out_done = 0
            hB_free = None
            g1_freeB = None
            h_ready = None
            g1_ready = None
            g2_ready = None
            hB_mm_last = 0

            @blk.sync
            def _(e):
                e.wait_ge(SO, phaseA_stores)
                e.dma_start(out=x1lnB[0][:].rearrange("p (k t) -> p k t", k=KC),
                            in_=x1ln_d[:, 0:T].rearrange("(k p) t -> p k t", p=P)).then_inc(SI, 16)
            C["i"] += 16
            x1ln_loadedB[0] = C["i"]

            sched = [("w12", mr) for mr in range(8)] + [("h1", mr) for mr in range(8)] + \
                    [("h2", mr) for mr in range(8)] + [("w3", mr) for mr in range(2)]

            for i in range(NT):
                if i + 1 < NT:
                    @blk.sync
                    def _(e, i=i):
                        wt(e, x1ln_freeB[(i + 1) % 2])
                        e.dma_start(out=x1lnB[(i + 1) % 2][:].rearrange("p (k t) -> p k t", k=KC),
                                    in_=x1ln_d[:, ts(i + 1, T)].rearrange("(k p) t -> p k t", p=P)).then_inc(SI, 16)
                    C["i"] += 16
                    x1ln_loadedB[(i + 1) % 2] = C["i"]

                slice_loaded = {}
                for idx, (kind, mr) in enumerate(sched):
                    j = idx % 2

                    if kind == "w3" and mr == 0:
                        @blk.sync
                        def _(e, i=i):
                            if x1_usedB > 0:
                                e.wait_ge(SV, x1_usedB)
                            e.dma_start(out=x1B[:].rearrange("p (k t) -> p k t", k=KC),
                                        in_=x1_d[:, ts(i, T)].rearrange("(k p) t -> p k t", p=P)).then_inc(SI, 16)
                        C["i"] += 16
                        x1_loadedB = C["i"]

                    @blk.sync
                    def _(e, kind=kind, mr=mr, j=j):
                        wt(e, slot_free[j])
                        need = {"w12": conv["w2"], "h1": conv["h1"], "h2": conv["h2"], "w3": conv["w3"]}[kind]
                        e.wait_ge(SW, need)
                        if kind == "w12":
                            e.dma_start(out=wsl[j][:, 0:KC * T].rearrange("p (k m) -> p k m", k=KC),
                                        in_=wb1_d[:, ts(mr, T)].rearrange("(k p) m -> p k m", p=P)).then_inc(SI, 16)
                            e.dma_start(out=wsl[j][:, KC * T:2 * KC * T].rearrange("p (k m) -> p k m", k=KC),
                                        in_=wb2_d[:, ts(mr, T)].rearrange("(k p) m -> p k m", p=P)).then_inc(SI, 16)
                        else:
                            src = {"h1": hb1_d, "h2": hb2_d, "w3": wb3_d}[kind]
                            e.dma_start(out=wsl[j][:].rearrange("p (k m) -> p k m", k=HC),
                                        in_=src[:, ts(mr, T)].rearrange("(k p) m -> p k m", p=P)).then_inc(SI, 16)
                    C["i"] += 32 if kind == "w12" else 16
                    slice_loaded[idx] = C["i"]

                    if kind == "w12":
                        for mc in range(4):
                            mcg = mr * 4 + mc
                            bA = mc % 2
                            bB = 2 + (mc % 2)
                            @blk.tensor
                            def _(e, j=j, mc=mc, bA=bA, bB=bB, idx=idx, i=i):
                                if mc == 0:
                                    e.wait_ge(SI, slice_loaded[idx])
                                    if idx == 0:
                                        e.wait_ge(SI, x1ln_loadedB[i % 2])
                                wt(e, pbank_free[bA])
                                for k in range(KC):
                                    ins = e.matmul(out=pb[bA][:],
                                                   lhsT=wsl[j][:, k * T + mc * P: k * T + (mc + 1) * P],
                                                   rhs=x1lnB[i % 2][:, ts(k, T)], start=(k == 0), stop=(k == KC - 1))
                                    if k == KC - 1:
                                        t_inc(ins)
                                wt(e, pbank_free[bB])
                                for k in range(KC):
                                    ins = e.matmul(out=pb[bB][:],
                                                   lhsT=wsl[j][:, KC * T + k * T + mc * P: KC * T + k * T + (mc + 1) * P],
                                                   rhs=x1lnB[i % 2][:, ts(k, T)], start=(k == 0), stop=(k == KC - 1))
                                    if k == KC - 1:
                                        t_inc(ins)
                            w2_done = C["t"]
                            w1_done = C["t"] - 1

                            @blk.scalar
                            def _(e, mc=mc, mcg=mcg, bA=bA, w1_done=w1_done):
                                e.wait_ge(STn, w1_done)
                                s_inc(e.activation(out=sA[mc % 2][:], in_=pb[bA][:], func=AF.Silu,
                                                   scale=s1_s[:, mcg:mcg + 1]))
                            silu_done = C["s"]
                            pbank_free[bA] = ("s", silu_done)

                            @blk.vector
                            def _(e, mc=mc, mcg=mcg, bB=bB, w2_done=w2_done, silu_done=silu_done):
                                e.wait_ge(STn, w2_done)
                                e.wait_ge(SS, silu_done)
                                if mcg == 0:
                                    wt(e, hB_free)
                                v_inc(e.scalar_tensor_tensor(out=hB[:, ts(mcg, T)], in0=pb[bB][:],
                                                             scalar=s2_s[:, mcg:mcg + 1], in1=sA[mc % 2][:],
                                                             op0=ALU.mult, op1=ALU.mult))
                            pbank_free[bB] = ("v", C["v"])
                        if mr == 7:
                            h_ready = ("v", C["v"])
                            x1ln_freeB[i % 2] = ("t", C["t"])
                    elif kind in ("h1", "h2"):
                        rhs_buf = hB if kind == "h1" else g1B
                        dst_buf = g1B if kind == "h1" else hB
                        scol = sh1_s if kind == "h1" else sh2_s
                        rhs_ready = h_ready if kind == "h1" else g1_ready
                        for mc in range(4):
                            mcg = mr * 4 + mc
                            bA = mc % 2
                            @blk.tensor
                            def _(e, j=j, mc=mc, bA=bA, idx=idx, rhs_buf=rhs_buf, rhs_ready=rhs_ready):
                                if mc == 0:
                                    e.wait_ge(SI, slice_loaded[idx])
                                    wt(e, rhs_ready)
                                wt(e, pbank_free[bA])
                                for k in range(HC):
                                    ins = e.matmul(out=pb[bA][:],
                                                   lhsT=wsl[j][:, k * T + mc * P: k * T + (mc + 1) * P],
                                                   rhs=rhs_buf[:, ts(k, T)], start=(k == 0), stop=(k == HC - 1))
                                    if k == HC - 1:
                                        t_inc(ins)
                            mm_done = C["t"]

                            @blk.scalar
                            def _(e, mcg=mcg, bA=bA, dst_buf=dst_buf, scol=scol, kind=kind, mm_done=mm_done):
                                e.wait_ge(STn, mm_done)
                                if mcg == 0:
                                    if kind == "h1":
                                        wt(e, g1_freeB)
                                    else:
                                        e.wait_ge(STn, hB_mm_last)
                                s_inc(e.activation(out=dst_buf[:, ts(mcg, T)], in_=pb[bA][:], func=AF.Gelu,
                                                   scale=scol[:, mcg:mcg + 1]))
                            pbank_free[bA] = ("s", C["s"])
                        if kind == "h1" and mr == 7:
                            g1_ready = ("s", C["s"])
                            hB_mm_last = C["t"]
                        if kind == "h2" and mr == 7:
                            g2_ready = ("s", C["s"])
                            g1_freeB = ("t", C["t"])
                    else:
                        for mc in range(4):
                            mcg = mr * 4 + mc
                            bA = mc % 2
                            @blk.tensor
                            def _(e, j=j, mc=mc, bA=bA, idx=idx, g2_ready=g2_ready):
                                if mc == 0:
                                    e.wait_ge(SI, slice_loaded[idx])
                                    wt(e, g2_ready)
                                wt(e, pbank_free[bA])
                                for k in range(HC):
                                    ins = e.matmul(out=pb[bA][:],
                                                   lhsT=wsl[j][:, k * T + mc * P: k * T + (mc + 1) * P],
                                                   rhs=hB[:, ts(k, T)], start=(k == 0), stop=(k == HC - 1))
                                    if k == HC - 1:
                                        t_inc(ins)
                            mm_done = C["t"]

                            @blk.vector
                            def _(e, mcg=mcg, bA=bA, mm_done=mm_done):
                                e.wait_ge(STn, mm_done)
                                if mcg == 0:
                                    e.wait_ge(SI, x1_loadedB)
                                    wt(e, out_stored)
                                v_inc(e.scalar_tensor_tensor(out=outB[:, ts(mcg, T)], in0=pb[bA][:],
                                                             scalar=s3_s[:, mcg:mcg + 1], in1=x1B[:, ts(mcg, T)],
                                                             op0=ALU.mult, op1=ALU.add))
                            pbank_free[bA] = ("v", C["v"])
                        if mr == 1:
                            out_done = C["v"]
                            hB_free = ("t", C["t"])
                            x1_usedB = C["v"]
                    slot_free[j] = ("t", C["t"])

                @blk.sync
                def _(e, i=i, out_done=out_done):
                    e.wait_ge(SV, out_done)
                    e.dma_start(out=outT[:, ts(i, T)].rearrange("(k p) t -> p k t", p=P),
                                in_=outB[:].rearrange("p (k t) -> p k t", k=KC)).then_inc(SO, 16)
                C["o"] += 16
                out_stored = ("o", C["o"])

            @blk.sync
            def _(e):
                e.wait_ge(SO, C["o"])

    return nc


_PROG_CACHE = {}

def _get_prog():
    if "nc" not in _PROG_CACHE:
        _PROG_CACHE["nc"] = build_program()
    return _PROG_CACHE["nc"]


def _consts():
    bf = ml_dtypes.bfloat16
    bo = np.zeros((P, P), bf); bo[0:64, 0:64] = 1; bo[64:128, 64:128] = 1
    dsel = np.zeros((P, DK), bf)
    for p in range(P):
        dsel[p, p % DK] = 1
    dn = np.full((P, 1), 1.0 / DK, bf)
    ones = np.ones((P, 1), bf)
    row = np.ones((1, P), bf)
    row64 = np.ones((P, P), bf)
    eps = np.full((P, 1), EPS, np.float32)
    return bo, dsel, dn, ones, row, row64, eps


def kernel(L0_attn_x, L0_attn_W_qkv, L0_attn_b_qkv, L0_attn_W_out, L0_attn_b_out,
           L0_ffn_w1_w, L0_ffn_w1_s, L0_ffn_w2_w, L0_ffn_w2_s,
           L0_ffn_h1_w, L0_ffn_h1_s, L0_ffn_h2_w, L0_ffn_h2_s,
           L0_ffn_w3_w, L0_ffn_w3_s):
    x = np.asarray(L0_attn_x)
    B = x.shape[0]
    wq = np.asarray(L0_attn_W_qkv)
    wq3 = wq.reshape(D, NH, 3, DK)
    wqkv = np.ascontiguousarray(wq3.transpose(0, 2, 1, 3).reshape(D, 3 * D))
    bq3 = np.asarray(L0_attn_b_qkv).reshape(NH, 3, DK)
    bqkv_perm = bq3.transpose(1, 0, 2).reshape(3 * D)
    bqkv = np.ascontiguousarray(bqkv_perm.reshape(3 * D // P, P).T)
    bout = np.ascontiguousarray(np.asarray(L0_attn_b_out).reshape(D // P, P).T)
    w1T = np.ascontiguousarray(np.asarray(L0_ffn_w1_w).T)
    w2T = np.ascontiguousarray(np.asarray(L0_ffn_w2_w).T)
    h1T = np.ascontiguousarray(np.asarray(L0_ffn_h1_w).T)
    h2T = np.ascontiguousarray(np.asarray(L0_ffn_h2_w).T)
    w3T = np.ascontiguousarray(np.asarray(L0_ffn_w3_w).T)
    s1 = np.ascontiguousarray(np.asarray(L0_ffn_w1_s).reshape(HC, P).T)
    s2 = np.ascontiguousarray(np.asarray(L0_ffn_w2_s).reshape(HC, P).T)
    sh1 = np.ascontiguousarray(np.asarray(L0_ffn_h1_s).reshape(HC, P).T)
    sh2 = np.ascontiguousarray(np.asarray(L0_ffn_h2_s).reshape(HC, P).T)
    s3 = np.ascontiguousarray(np.asarray(L0_ffn_w3_s).reshape(D // P, P).T)
    bo, dsel, dn, ones, row, row64, eps = _consts()

    shared = dict(wqkv=wqkv, bqkv=bqkv, wout=np.ascontiguousarray(np.asarray(L0_attn_W_out)),
                  bout=bout, w1T=w1T, w2T=w2T, h1T=h1T, h2T=h2T, w3T=w3T,
                  s1=s1, s2=s2, sh1=sh1, sh2=sh2, s3=s3,
                  c_bo=bo, c_dsel=dsel, c_dn=dn, c_ones=ones, c_row=row, c_row64=row64, c_eps=eps)

    in_maps = []
    for c in range(B):
        m = dict(shared)
        m["xT"] = np.ascontiguousarray(x[c].T)
        in_maps.append(m)

    nc = _get_prog()
    res = run_bass_kernel_spmd(nc, in_maps, core_ids=list(range(N_CORES)))
    out = np.stack([np.asarray(r["outT"], dtype=np.float32).T for r in res.results], axis=0)
    return out


# version 02
# speedup vs baseline: 1.2582x; 1.2582x over previous
import numpy as np
import ml_dtypes
from contextlib import ExitStack

import concourse.bass as bass
import concourse.mybir as mybir
from concourse.bass_utils import run_bass_kernel_spmd

F32 = mybir.dt.float32
BF16 = mybir.dt.bfloat16
AF = mybir.ActivationFunctionType
ALU = mybir.AluOpType
ts = bass.ts

P = 128
D = 1024
DH = 4096
NH, DK = 16, 64
S = 4096
T = 512
NT = S // T
KC = D // P
HC = DH // P
N_CORES = 8
EPS = 1e-5


def build_program():
    nc = bass.Bass("TRN2", target_bir_lowering=False)

    xT = nc.dram_tensor("xT", [D, S], F32, kind="ExternalInput").ap()
    wqkv = nc.dram_tensor("wqkv", [D, 3 * D], F32, kind="ExternalInput").ap()
    bqkv = nc.dram_tensor("bqkv", [1, 3 * D], F32, kind="ExternalInput").ap()
    wout = nc.dram_tensor("wout", [D, D], F32, kind="ExternalInput").ap()
    bout = nc.dram_tensor("bout", [P, D // P], F32, kind="ExternalInput").ap()
    w1T = nc.dram_tensor("w1T", [D, DH], F32, kind="ExternalInput").ap()
    w2T = nc.dram_tensor("w2T", [D, DH], F32, kind="ExternalInput").ap()
    h1T = nc.dram_tensor("h1T", [DH, DH], F32, kind="ExternalInput").ap()
    h2T = nc.dram_tensor("h2T", [DH, DH], F32, kind="ExternalInput").ap()
    w3T = nc.dram_tensor("w3T", [DH, D], F32, kind="ExternalInput").ap()
    s1 = nc.dram_tensor("s1", [P, HC], F32, kind="ExternalInput").ap()
    s2 = nc.dram_tensor("s2", [P, HC], F32, kind="ExternalInput").ap()
    sh1 = nc.dram_tensor("sh1", [P, HC], F32, kind="ExternalInput").ap()
    sh2 = nc.dram_tensor("sh2", [P, HC], F32, kind="ExternalInput").ap()
    s3 = nc.dram_tensor("s3", [P, D // P], F32, kind="ExternalInput").ap()
    c_dsel = nc.dram_tensor("c_dsel", [P, DK], BF16, kind="ExternalInput").ap()
    c_bdsel = nc.dram_tensor("c_bdsel", [P, KC * NH], BF16, kind="ExternalInput").ap()
    c_gsel = nc.dram_tensor("c_gsel", [NH, KC * P], BF16, kind="ExternalInput").ap()
    c_ones = nc.dram_tensor("c_ones", [P, 1], BF16, kind="ExternalInput").ap()
    c_row = nc.dram_tensor("c_row", [1, P], BF16, kind="ExternalInput").ap()
    c_row64 = nc.dram_tensor("c_row64", [P, P], BF16, kind="ExternalInput").ap()
    c_onesT = nc.dram_tensor("c_onesT", [1, T], BF16, kind="ExternalInput").ap()
    c_eps = nc.dram_tensor("c_eps", [P, 1], F32, kind="ExternalInput").ap()

    outT = nc.dram_tensor("outT", [D, S], F32, kind="ExternalOutput").ap()

    x1_d = nc.dram_tensor("x1_d", [D, S], F32, kind="Internal").ap()
    x1ln_d = nc.dram_tensor("x1ln_d", [D, S], BF16, kind="Internal").ap()
    wb1_d = nc.dram_tensor("wb1_d", [D, DH], BF16, kind="Internal").ap()
    wb2_d = nc.dram_tensor("wb2_d", [D, DH], BF16, kind="Internal").ap()
    hb1_d = nc.dram_tensor("hb1_d", [DH, DH], BF16, kind="Internal").ap()
    hb2_d = nc.dram_tensor("hb2_d", [DH, DH], BF16, kind="Internal").ap()
    wb3_d = nc.dram_tensor("wb3_d", [DH, D], BF16, kind="Internal").ap()

    with ExitStack() as g:
        E = g.enter_context
        dsel_s = E(nc.sbuf_tensor("dsel_s", [P, DK], BF16))
        bdsel_s = E(nc.sbuf_tensor("bdsel_s", [P, KC * NH], BF16))
        gsel_s = E(nc.sbuf_tensor("gsel_s", [NH, KC * P], BF16))
        ones_s = E(nc.sbuf_tensor("ones_s", [P, 1], BF16))
        row_s = E(nc.sbuf_tensor("row_s", [1, P], BF16))
        row64_s = E(nc.sbuf_tensor("row64_s", [P, P], BF16))
        onesT_s = E(nc.sbuf_tensor("onesT_s", [1, T], BF16))
        eps_s = E(nc.sbuf_tensor("eps_s", [P, 1], F32))
        bqkv_s = E(nc.sbuf_tensor("bqkv_s", [1, 3 * D], BF16))
        bout_s = E(nc.sbuf_tensor("bout_s", [P, D // P], F32))
        s1_s = E(nc.sbuf_tensor("s1_s", [P, HC], F32))
        s2_s = E(nc.sbuf_tensor("s2_s", [P, HC], F32))
        sh1_s = E(nc.sbuf_tensor("sh1_s", [P, HC], F32))
        sh2_s = E(nc.sbuf_tensor("sh2_s", [P, HC], F32))
        s3_s = E(nc.sbuf_tensor("s3_s", [P, D // P], F32))

        pb = [E(nc.psum_tensor(f"pb{i}", [P, T], F32)) for i in range(8)]

        SW = E(nc.semaphore("SW"))
        SI = E(nc.semaphore("SI"))
        SO = E(nc.semaphore("SO"))
        STn = E(nc.semaphore("STn"))
        SV = E(nc.semaphore("SV"))
        SS = E(nc.semaphore("SS"))
        blk = E(nc.Block())
        C = dict(w=0, i=0, o=0, t=0, v=0, s=0)
        SEMS = {"t": STn, "v": SV, "s": SS, "i": SI, "o": SO, "w": SW}

        def t_inc(instr):
            C["t"] += 1
            return instr.then_inc(STn, 1)

        def v_inc(instr):
            C["v"] += 1
            return instr.then_inc(SV, 1)

        def s_inc(instr):
            C["s"] += 1
            return instr.then_inc(SS, 1)

        def wt(e, tok):
            if tok is None:
                return
            if tok[1] > 0:
                e.wait_ge(SEMS[tok[0]], tok[1])

        @blk.sync
        def _(e):
            for dst, src in [(dsel_s, c_dsel), (bdsel_s, c_bdsel), (gsel_s, c_gsel),
                             (ones_s, c_ones), (row_s, c_row), (row64_s, c_row64),
                             (onesT_s, c_onesT), (eps_s, c_eps),
                             (bout_s, bout),
                             (s1_s, s1), (s2_s, s2), (sh1_s, sh1), (sh2_s, sh2), (s3_s, s3)]:
                e.dma_start(out=dst[:], in_=src).then_inc(SI, 16)
                C["i"] += 16
        consts_loaded = C["i"]

        @blk.tensor
        def _(e):
            e.wait_ge(SI, consts_loaded)
        @blk.scalar
        def _(e):
            e.wait_ge(SI, consts_loaded)
        @blk.vector
        def _(e):
            e.wait_ge(SI, consts_loaded)

        with ExitStack() as pa:
            EA = pa.enter_context
            wqkv_s = EA(nc.sbuf_tensor("wqkv_s", [P, KC * 3 * D], BF16))
            wout_s = EA(nc.sbuf_tensor("wout_s", [P, KC * D], BF16))
            xbuf = [EA(nc.sbuf_tensor(f"xbuf{j}", [P, KC * T], F32)) for j in range(2)]
            xbb = [EA(nc.sbuf_tensor(f"xbb{j}", [P, T], BF16)) for j in range(2)]
            sqb = [EA(nc.sbuf_tensor(f"sqb{j}", [P, T], BF16)) for j in range(2)]
            t1f = EA(nc.sbuf_tensor("t1f", [P, T], F32))
            rows = EA(nc.sbuf_tensor("rows", [1, 5 * T], F32))
            rowb = EA(nc.sbuf_tensor("rowb", [1, 2 * T], BF16))
            xln_s = EA(nc.sbuf_tensor("xln_s", [P, KC * T], BF16))
            q_s = EA(nc.sbuf_tensor("q_s", [P, KC * T], BF16))
            k_s = EA(nc.sbuf_tensor("k_s", [P, KC * T], BF16))
            v_s = EA(nc.sbuf_tensor("v_s", [P, KC * T], BF16))
            qrep = EA(nc.sbuf_tensor("qrep", [P, T], BF16))
            tmpa = [EA(nc.sbuf_tensor(f"tmpa{j}", [P, KC * T], BF16)) for j in range(2)]
            esb = [EA(nc.sbuf_tensor(f"esb{j}", [NH, T], BF16)) for j in range(2)]
            tmp2 = [EA(nc.sbuf_tensor(f"tmp2_{j}", [P, T], BF16)) for j in range(2)]
            rband = EA(nc.sbuf_tensor("rband", [33, T], F32))
            rbandb = EA(nc.sbuf_tensor("rbandb", [33, 2 * T], BF16))
            ctxb = EA(nc.sbuf_tensor("ctxb", [P, KC * T], BF16))
            x1_s = EA(nc.sbuf_tensor("x1_s", [P, KC * T], F32))

            @blk.gpsimd
            def _(e):
                e.dma_start(out=wqkv_s[:].rearrange("p (k m) -> p k m", k=KC),
                            in_=wqkv.rearrange("(k p) m -> p k m", p=P)).then_inc(SW, 16)
                e.dma_start(out=wout_s[:].rearrange("p (k m) -> p k m", k=KC),
                            in_=wout.rearrange("(k p) m -> p k m", p=P)).then_inc(SW, 16)
                e.dma_start(out=bqkv_s[:], in_=bqkv).then_inc(SW, 16)
            C["w"] += 48
            qkvw_loaded = C["w"]

            conv = {}
            @blk.gpsimd
            def _(e):
                for name, src, dst, nrows in [("w1", w1T, wb1_d, D), ("w2", w2T, wb2_d, D),
                                              ("h1", h1T, hb1_d, DH), ("h2", h2T, hb2_d, DH),
                                              ("w3", w3T, wb3_d, DH)]:
                    step = 256
                    for r0 in range(0, nrows, step):
                        e.dma_start(out=dst[r0:r0 + step, :], in_=src[r0:r0 + step, :]).then_inc(SW, 16)
                        C["w"] += 16
                    conv[name] = C["w"]

            x_free = [None, None]
            x_loaded = [0, 0]
            x1_store = None
            x1ln_store = None
            rowb_free = None
            xln_free = None
            qkv_free = None
            ctxb_free = None
            pbank_free = [None] * 8

            @blk.sync
            def _(e):
                e.dma_start(out=xbuf[0][:].rearrange("p (k t) -> p k t", k=KC),
                            in_=xT[:, 0:T].rearrange("(k p) t -> p k t", p=P)).then_inc(SI, 16)
            C["i"] += 16
            x_loaded[0] = C["i"]

            def emit_ln(src_f32, dst_bf):
                nonlocal rowb_free
                stat_bank, b1, b2 = 4, 2, 3
                mm_tok = [0, 0]
                for c in range(KC):
                    @blk.vector
                    def _(e, c=c):
                        if mm_tok[c % 2] > 0:
                            e.wait_ge(STn, mm_tok[c % 2])
                        e.tensor_copy(out=xbb[c % 2][:], in_=src_f32[:, ts(c, T)])
                        v_inc(e.tensor_tensor(out=sqb[c % 2][:], in0=xbb[c % 2][:], in1=xbb[c % 2][:], op=ALU.mult))
                    sq_done = C["v"]

                    @blk.tensor
                    def _(e, c=c, sq_done=sq_done):
                        e.wait_ge(SV, sq_done)
                        if c == 0:
                            wt(e, pbank_free[stat_bank])
                        e.matmul(out=pb[stat_bank][0:1, :], lhsT=ones_s[:], rhs=xbb[c % 2][:],
                                 start=(c == 0), stop=(c == KC - 1), skip_group_check=True)
                        t_inc(e.matmul(out=pb[stat_bank][32:33, :], lhsT=ones_s[:], rhs=sqb[c % 2][:],
                                       start=(c == 0), stop=(c == KC - 1), skip_group_check=True,
                                       tile_position=(0, 32)))
                    mm_tok[c % 2] = C["t"]
                stats_done = C["t"]

                m_row = rows[:, 0:T]; v_row = rows[:, T:2 * T]; msq_row = rows[:, 2 * T:3 * T]
                r_row = rows[:, 3 * T:4 * T]; mr_row = rows[:, 4 * T:5 * T]

                @blk.vector
                def _(e):
                    e.wait_ge(STn, stats_done)
                    e.tensor_scalar(out=m_row, in0=pb[stat_bank][0:1, :], scalar1=1.0 / D, scalar2=None, op0=ALU.mult)
                    e.tensor_scalar(out=v_row, in0=pb[stat_bank][32:33, :], scalar1=1.0 / D, scalar2=None, op0=ALU.mult)
                    e.tensor_tensor(out=msq_row, in0=m_row, in1=m_row, op=ALU.mult)
                    v_inc(e.tensor_tensor(out=v_row, in0=v_row, in1=msq_row, op=ALU.subtract))
                var_done = C["v"]
                pbank_free[stat_bank] = ("v", var_done)

                @blk.scalar
                def _(e):
                    e.wait_ge(SV, var_done)
                    e.activation(out=r_row, in_=v_row, func=AF.Ln, bias=eps_s[0:1, :])
                    s_inc(e.activation(out=r_row, in_=r_row, func=AF.Exp, scale=-0.5))
                r_done = C["s"]

                @blk.vector
                def _(e):
                    e.wait_ge(SS, r_done)
                    wt(e, rowb_free)
                    e.tensor_tensor(out=mr_row, in0=m_row, in1=r_row, op=ALU.mult)
                    e.tensor_copy(out=rowb[:, 0:T], in_=r_row)
                    v_inc(e.tensor_copy(out=rowb[:, T:2 * T], in_=mr_row))
                rowb_done = C["v"]

                @blk.tensor
                def _(e):
                    e.wait_ge(SV, rowb_done)
                    wt(e, pbank_free[b1])
                    wt(e, pbank_free[b2])
                    e.matmul(out=pb[b1][:], lhsT=row_s[:], rhs=rowb[:, 0:T], start=True, stop=True)
                    t_inc(e.matmul(out=pb[b2][:], lhsT=row_s[:], rhs=rowb[:, T:2 * T], start=True, stop=True))
                bcast_done = C["t"]
                rowb_free = ("t", bcast_done)

                @blk.vector
                def _(e):
                    e.wait_ge(STn, bcast_done)
                    ins = None
                    for c in range(KC):
                        e.tensor_tensor(out=t1f[:], in0=src_f32[:, ts(c, T)], in1=pb[b1][:], op=ALU.mult)
                        ins = e.tensor_tensor(out=dst_bf[:, ts(c, T)], in0=t1f[:], in1=pb[b2][:], op=ALU.subtract)
                    v_inc(ins)
                apply_done = C["v"]
                pbank_free[b1] = ("v", apply_done)
                pbank_free[b2] = ("v", apply_done)
                return apply_done

            tmpa_free = [0, 0]
            esb_free = [0, 0]
            psS_free = [None, None]
            psU_free = [None, None]
            rbandb_free = [0, 0]
            tmp2_free = [0, 0]
            psE_free = [0, 0]

            for i in range(NT):
                xb_i = xbuf[i % 2]

                if i + 1 < NT:
                    @blk.sync
                    def _(e, i=i):
                        wt(e, x_free[(i + 1) % 2])
                        e.dma_start(out=xbuf[(i + 1) % 2][:].rearrange("p (k t) -> p k t", k=KC),
                                    in_=xT[:, ts(i + 1, T)].rearrange("(k p) t -> p k t", p=P)).then_inc(SI, 16)
                    C["i"] += 16
                    x_loaded[(i + 1) % 2] = C["i"]

                @blk.vector
                def _(e, i=i):
                    e.wait_ge(SI, x_loaded[i % 2])
                    wt(e, xln_free)
                    wt(e, x1ln_store)
                    e.engine_nop()
                ln1_done = emit_ln(xb_i, xln_s)
                psE_free[0] = max(psE_free[0], ln1_done)
                psE_free[1] = max(psE_free[1], ln1_done)

                mm_done_at = {}
                for m in range(3 * D // P):
                    bank = m % 2
                    @blk.tensor
                    def _(e, m=m, bank=bank):
                        if m == 0:
                            e.wait_ge(SV, ln1_done)
                            e.wait_ge(SW, qkvw_loaded)
                        wt(e, pbank_free[bank])
                        for k in range(KC):
                            e.matmul(out=pb[bank][:],
                                     lhsT=wqkv_s[:, k * (3 * D) + m * P: k * (3 * D) + (m + 1) * P],
                                     rhs=xln_s[:, ts(k, T)], start=(k == 0), stop=False)
                        t_inc(e.matmul(out=pb[bank][:], lhsT=bqkv_s[:, m * P:(m + 1) * P],
                                       rhs=onesT_s[:], start=False, stop=True))
                    mm_done_at[m] = C["t"]

                    dst = (q_s, k_s, v_s)[m // KC]
                    col = (m % KC) * T
                    @blk.scalar
                    def _(e, m=m, bank=bank, dst=dst, col=col):
                        e.wait_ge(STn, mm_done_at[m])
                        if m == 0:
                            wt(e, qkv_free)
                        s_inc(e.activation(out=dst[:, col:col + T], in_=pb[bank][:], func=AF.Copy))
                    pbank_free[bank] = ("s", C["s"])
                qkv_done = C["s"]
                xln_free = ("t", mm_done_at[3 * D // P - 1])

                uevac = {}

                for h in range(NH):
                    c_q, r_q = h // 2, (h % 2) * DK
                    sb = 6 + (h % 2)
                    ub = 4 + (h % 2)

                    @blk.vector
                    def _(e, h=h, c_q=c_q, r_q=r_q):
                        if h == 0:
                            e.wait_ge(SS, qkv_done)
                        if tmpa_free[h % 2] > 0:
                            e.wait_ge(STn, tmpa_free[h % 2])
                        e.tensor_copy(out=qrep[0:DK, :], in_=q_s[r_q:r_q + DK, ts(c_q, T)])
                        e.tensor_copy(out=qrep[DK:P, :], in_=q_s[r_q:r_q + DK, ts(c_q, T)])
                        ins = None
                        for c in range(KC):
                            ins = e.tensor_tensor(out=tmpa[h % 2][:, ts(c, T)], in0=k_s[:, ts(c, T)],
                                                  in1=qrep[:], op=ALU.mult)
                        v_inc(ins)
                    tmp_ready = C["v"]

                    @blk.tensor
                    def _(e, h=h, sb=sb, tmp_ready=tmp_ready):
                        e.wait_ge(SV, tmp_ready)
                        wt(e, psS_free[h % 2])
                        ins = None
                        for c in range(KC):
                            ins = e.matmul(out=pb[sb][0:NH, :], lhsT=bdsel_s[:, ts(c, NH)],
                                           rhs=tmpa[h % 2][:, ts(c, T)],
                                           start=(c == 0), stop=(c == KC - 1), skip_group_check=True)
                        t_inc(ins)
                    sc_done = C["t"]
                    tmpa_free[h % 2] = sc_done

                    @blk.scalar
                    def _(e, h=h, sb=sb, sc_done=sc_done):
                        e.wait_ge(STn, sc_done)
                        if esb_free[h % 2] > 0:
                            e.wait_ge(STn, esb_free[h % 2])
                        e.activation(out=rband[0:1, 0:1], in_=eps_s[0:1, :], func=AF.Ln)
                        s_inc(e.activation(out=esb[h % 2][:], in_=pb[sb][0:NH, :], func=AF.Exp, scale=0.125))
                    exp_done = C["s"]

                    den_done = None
                    for j in range(4):
                        cpair = (2 * j, 2 * j + 1)
                        erep_tok = {}
                        @blk.tensor
                        def _(e, h=h, j=j, cpair=cpair, sb=sb, exp_done=exp_done, erep_tok=erep_tok):
                            if j == 0:
                                e.wait_ge(SS, exp_done)
                            for c in cpair:
                                gp = c % 2
                                if psE_free[gp] > 0:
                                    e.wait_ge(SV, psE_free[gp])
                                t_inc(e.matmul(out=pb[2 + gp][:], lhsT=gsel_s[:, ts(c, P)],
                                               rhs=esb[h % 2][:], start=True, stop=True))
                                erep_tok[c] = C["t"]
                            if j == 0:
                                t_inc(e.matmul(out=pb[sb][32:33, :], lhsT=ones_s[0:NH, :],
                                               rhs=esb[h % 2][:], start=True, stop=True,
                                               tile_position=(0, 32), skip_group_check=True))
                        if j == 0:
                            den_done = C["t"]

                        t2_tok = {}
                        @blk.vector
                        def _(e, h=h, cpair=cpair, erep_tok=erep_tok, t2_tok=t2_tok):
                            for c in cpair:
                                gp = c % 2
                                e.wait_ge(STn, erep_tok[c])
                                if tmp2_free[gp] > 0:
                                    e.wait_ge(STn, tmp2_free[gp])
                                v_inc(e.tensor_tensor(out=tmp2[gp][:], in0=pb[2 + gp][:],
                                                      in1=v_s[:, ts(c, T)], op=ALU.mult))
                                t2_tok[c] = C["v"]
                                psE_free[gp] = C["v"]

                        @blk.tensor
                        def _(e, h=h, cpair=cpair, ub=ub, t2_tok=t2_tok):
                            for c in cpair:
                                gp = c % 2
                                e.wait_ge(SV, t2_tok[c])
                                if c == 0:
                                    wt(e, psU_free[h % 2])
                                t_inc(e.matmul(out=pb[ub][0:DK, :], lhsT=dsel_s[:], rhs=tmp2[gp][:],
                                               start=(c == 0), stop=(c == KC - 1), skip_group_check=True))
                                tmp2_free[gp] = C["t"]
                    u_done = C["t"]
                    esb_free[h % 2] = u_done

                    @blk.scalar
                    def _(e, h=h, sb=sb, ub=ub, c_q=c_q, r_q=r_q, den_done=den_done, u_done=u_done):
                        e.wait_ge(STn, den_done)
                        if rbandb_free[h % 2] > 0:
                            e.wait_ge(STn, rbandb_free[h % 2])
                        e.activation(out=rband[32:33, :], in_=pb[sb][32:33, :], func=AF.Ln)
                        s_inc(e.activation(out=rbandb[32:33, ts(h % 2, T)], in_=rband[32:33, :],
                                           func=AF.Exp, scale=-1.0))
                        e.wait_ge(STn, u_done)
                        if h < 2:
                            wt(e, ctxb_free)
                        s_inc(e.activation(out=ctxb[r_q:r_q + DK, ts(c_q, T)], in_=pb[ub][0:DK, :], func=AF.Copy))
                    r_done = C["s"] - 1
                    uevac[h] = C["s"]
                    psU_free[h % 2] = ("s", uevac[h])

                    @blk.tensor
                    def _(e, h=h, sb=sb, r_done=r_done):
                        e.wait_ge(SS, r_done)
                        t_inc(e.matmul(out=pb[sb][:], lhsT=row64_s[32:33, :],
                                       rhs=rbandb[32:33, ts(h % 2, T)], start=True, stop=True))
                    rrep_done = C["t"]
                    rbandb_free[h % 2] = rrep_done

                    @blk.vector
                    def _(e, h=h, sb=sb, c_q=c_q, r_q=r_q, rrep_done=rrep_done):
                        e.wait_ge(STn, rrep_done)
                        e.wait_ge(SS, uevac[h])
                        if h < 2:
                            wt(e, ctxb_free)
                        v_inc(e.tensor_tensor(out=ctxb[r_q:r_q + DK, ts(c_q, T)],
                                              in0=ctxb[r_q:r_q + DK, ts(c_q, T)],
                                              in1=pb[sb][r_q:r_q + DK, :], op=ALU.mult))
                    psS_free[h % 2] = ("v", C["v"])

                ctx_done = C["v"]
                qkv_free = ("v", ctx_done)
                pbank_free[4] = psU_free[0]
                pbank_free[5] = psU_free[1]
                pbank_free[2] = ("v", psE_free[0])
                pbank_free[3] = ("v", psE_free[1])

                for m in range(KC):
                    bank = m % 2
                    @blk.tensor
                    def _(e, m=m, bank=bank):
                        if m == 0:
                            e.wait_ge(SV, ctx_done)
                        wt(e, pbank_free[bank])
                        ins = None
                        for k in range(KC):
                            ins = e.matmul(out=pb[bank][:],
                                           lhsT=wout_s[:, k * D + m * P: k * D + (m + 1) * P],
                                           rhs=ctxb[:, ts(k, T)], start=(k == 0), stop=(k == KC - 1))
                        t_inc(ins)
                    wmm = C["t"]
                    if m == KC - 1:
                        ctxb_free = ("t", wmm)

                    @blk.vector
                    def _(e, m=m, bank=bank, wmm=wmm):
                        e.wait_ge(STn, wmm)
                        if m == 0:
                            wt(e, x1_store)
                        v_inc(e.scalar_tensor_tensor(out=x1_s[:, ts(m, T)], in0=pb[bank][:],
                                                     scalar=bout_s[:, m:m + 1], in1=xb_i[:, ts(m, T)],
                                                     op0=ALU.add, op1=ALU.add))
                    pbank_free[bank] = ("v", C["v"])
                x1_done = C["v"]
                x_free[i % 2] = ("v", x1_done)

                @blk.sync
                def _(e, i=i):
                    e.wait_ge(SV, x1_done)
                    e.dma_start(out=x1_d[:, ts(i, T)].rearrange("(k p) t -> p k t", p=P),
                                in_=x1_s[:].rearrange("p (k t) -> p k t", k=KC)).then_inc(SO, 16)
                C["o"] += 16
                x1_store = ("o", C["o"])

                @blk.vector
                def _(e):
                    wt(e, x1ln_store)
                    wt(e, xln_free)
                    e.engine_nop()
                ln2_done = emit_ln(x1_s, xln_s)

                @blk.sync
                def _(e, i=i, ln2_done=ln2_done):
                    e.wait_ge(SV, ln2_done)
                    e.dma_start(out=x1ln_d[:, ts(i, T)].rearrange("(k p) t -> p k t", p=P),
                                in_=xln_s[:].rearrange("p (k t) -> p k t", k=KC)).then_inc(SO, 16)
                C["o"] += 16
                x1ln_store = ("o", C["o"])

            phaseA_stores = C["o"]

        with ExitStack() as pbx:
            EB = pbx.enter_context
            wsl = [EB(nc.sbuf_tensor(f"wsl{j}", [P, HC * T], BF16)) for j in range(2)]
            x1lnB = [EB(nc.sbuf_tensor(f"x1lnB{j}", [P, KC * T], BF16)) for j in range(2)]
            hB = EB(nc.sbuf_tensor("hB", [P, HC * T], BF16))
            g1B = EB(nc.sbuf_tensor("g1B", [P, HC * T], BF16))
            sA = [EB(nc.sbuf_tensor(f"sA{j}", [P, T], BF16)) for j in range(2)]
            x1B = EB(nc.sbuf_tensor("x1B", [P, KC * T], F32))
            outB = EB(nc.sbuf_tensor("outB", [P, KC * T], F32))

            slot_free = [None, None]
            x1ln_loadedB = [0, 0]
            x1ln_freeB = [None, None]
            x1_loadedB = 0
            x1_usedB = 0
            out_stored = None
            out_done = 0
            hB_free = None
            g1_freeB = None
            h_ready = None
            g1_ready = None
            g2_ready = None
            hB_mm_last = 0

            @blk.sync
            def _(e):
                e.wait_ge(SO, phaseA_stores)
                e.dma_start(out=x1lnB[0][:].rearrange("p (k t) -> p k t", k=KC),
                            in_=x1ln_d[:, 0:T].rearrange("(k p) t -> p k t", p=P)).then_inc(SI, 16)
            C["i"] += 16
            x1ln_loadedB[0] = C["i"]

            sched = [("w12", mr) for mr in range(8)] + [("h1", mr) for mr in range(8)] + \
                    [("h2", mr) for mr in range(8)] + [("w3", mr) for mr in range(2)]

            for i in range(NT):
                if i + 1 < NT:
                    @blk.sync
                    def _(e, i=i):
                        wt(e, x1ln_freeB[(i + 1) % 2])
                        e.dma_start(out=x1lnB[(i + 1) % 2][:].rearrange("p (k t) -> p k t", k=KC),
                                    in_=x1ln_d[:, ts(i + 1, T)].rearrange("(k p) t -> p k t", p=P)).then_inc(SI, 16)
                    C["i"] += 16
                    x1ln_loadedB[(i + 1) % 2] = C["i"]

                slice_loaded = {}
                for idx, (kind, mr) in enumerate(sched):
                    j = idx % 2

                    if kind == "w3" and mr == 0:
                        @blk.sync
                        def _(e, i=i):
                            if x1_usedB > 0:
                                e.wait_ge(SV, x1_usedB)
                            e.dma_start(out=x1B[:].rearrange("p (k t) -> p k t", k=KC),
                                        in_=x1_d[:, ts(i, T)].rearrange("(k p) t -> p k t", p=P)).then_inc(SI, 16)
                        C["i"] += 16
                        x1_loadedB = C["i"]

                    @blk.sync
                    def _(e, kind=kind, mr=mr, j=j):
                        wt(e, slot_free[j])
                        need = {"w12": conv["w2"], "h1": conv["h1"], "h2": conv["h2"], "w3": conv["w3"]}[kind]
                        e.wait_ge(SW, need)
                        if kind == "w12":
                            e.dma_start(out=wsl[j][:, 0:KC * T].rearrange("p (k m) -> p k m", k=KC),
                                        in_=wb1_d[:, ts(mr, T)].rearrange("(k p) m -> p k m", p=P)).then_inc(SI, 16)
                            e.dma_start(out=wsl[j][:, KC * T:2 * KC * T].rearrange("p (k m) -> p k m", k=KC),
                                        in_=wb2_d[:, ts(mr, T)].rearrange("(k p) m -> p k m", p=P)).then_inc(SI, 16)
                        else:
                            src = {"h1": hb1_d, "h2": hb2_d, "w3": wb3_d}[kind]
                            e.dma_start(out=wsl[j][:].rearrange("p (k m) -> p k m", k=HC),
                                        in_=src[:, ts(mr, T)].rearrange("(k p) m -> p k m", p=P)).then_inc(SI, 16)
                    C["i"] += 32 if kind == "w12" else 16
                    slice_loaded[idx] = C["i"]

                    if kind == "w12":
                        for mc in range(4):
                            mcg = mr * 4 + mc
                            bA = mc % 2
                            bB = 2 + (mc % 2)
                            @blk.tensor
                            def _(e, j=j, mc=mc, bA=bA, bB=bB, idx=idx, i=i):
                                if mc == 0:
                                    e.wait_ge(SI, slice_loaded[idx])
                                    if idx == 0:
                                        e.wait_ge(SI, x1ln_loadedB[i % 2])
                                wt(e, pbank_free[bA])
                                for k in range(KC):
                                    ins = e.matmul(out=pb[bA][:],
                                                   lhsT=wsl[j][:, k * T + mc * P: k * T + (mc + 1) * P],
                                                   rhs=x1lnB[i % 2][:, ts(k, T)], start=(k == 0), stop=(k == KC - 1))
                                    if k == KC - 1:
                                        t_inc(ins)
                                wt(e, pbank_free[bB])
                                for k in range(KC):
                                    ins = e.matmul(out=pb[bB][:],
                                                   lhsT=wsl[j][:, KC * T + k * T + mc * P: KC * T + k * T + (mc + 1) * P],
                                                   rhs=x1lnB[i % 2][:, ts(k, T)], start=(k == 0), stop=(k == KC - 1))
                                    if k == KC - 1:
                                        t_inc(ins)
                            w2_done = C["t"]
                            w1_done = C["t"] - 1

                            @blk.scalar
                            def _(e, mc=mc, mcg=mcg, bA=bA, w1_done=w1_done):
                                e.wait_ge(STn, w1_done)
                                s_inc(e.activation(out=sA[mc % 2][:], in_=pb[bA][:], func=AF.Silu,
                                                   scale=s1_s[:, mcg:mcg + 1]))
                            silu_done = C["s"]
                            pbank_free[bA] = ("s", silu_done)

                            @blk.vector
                            def _(e, mc=mc, mcg=mcg, bB=bB, w2_done=w2_done, silu_done=silu_done):
                                e.wait_ge(STn, w2_done)
                                e.wait_ge(SS, silu_done)
                                if mcg == 0:
                                    wt(e, hB_free)
                                v_inc(e.scalar_tensor_tensor(out=hB[:, ts(mcg, T)], in0=pb[bB][:],
                                                             scalar=s2_s[:, mcg:mcg + 1], in1=sA[mc % 2][:],
                                                             op0=ALU.mult, op1=ALU.mult))
                            pbank_free[bB] = ("v", C["v"])
                        if mr == 7:
                            h_ready = ("v", C["v"])
                            x1ln_freeB[i % 2] = ("t", C["t"])
                    elif kind in ("h1", "h2"):
                        rhs_buf = hB if kind == "h1" else g1B
                        dst_buf = g1B if kind == "h1" else hB
                        scol = sh1_s if kind == "h1" else sh2_s
                        rhs_ready = h_ready if kind == "h1" else g1_ready
                        for mc in range(4):
                            mcg = mr * 4 + mc
                            bA = mc % 2
                            @blk.tensor
                            def _(e, j=j, mc=mc, bA=bA, idx=idx, rhs_buf=rhs_buf, rhs_ready=rhs_ready):
                                if mc == 0:
                                    e.wait_ge(SI, slice_loaded[idx])
                                    wt(e, rhs_ready)
                                wt(e, pbank_free[bA])
                                for k in range(HC):
                                    ins = e.matmul(out=pb[bA][:],
                                                   lhsT=wsl[j][:, k * T + mc * P: k * T + (mc + 1) * P],
                                                   rhs=rhs_buf[:, ts(k, T)], start=(k == 0), stop=(k == HC - 1))
                                    if k == HC - 1:
                                        t_inc(ins)
                            mm_done = C["t"]

                            @blk.scalar
                            def _(e, mcg=mcg, bA=bA, dst_buf=dst_buf, scol=scol, kind=kind, mm_done=mm_done):
                                e.wait_ge(STn, mm_done)
                                if mcg == 0:
                                    if kind == "h1":
                                        wt(e, g1_freeB)
                                    else:
                                        e.wait_ge(STn, hB_mm_last)
                                s_inc(e.activation(out=dst_buf[:, ts(mcg, T)], in_=pb[bA][:], func=AF.Gelu,
                                                   scale=scol[:, mcg:mcg + 1]))
                            pbank_free[bA] = ("s", C["s"])
                        if kind == "h1" and mr == 7:
                            g1_ready = ("s", C["s"])
                            hB_mm_last = C["t"]
                        if kind == "h2" and mr == 7:
                            g2_ready = ("s", C["s"])
                            g1_freeB = ("t", C["t"])
                    else:
                        for mc in range(4):
                            mcg = mr * 4 + mc
                            bA = mc % 2
                            @blk.tensor
                            def _(e, j=j, mc=mc, bA=bA, idx=idx, g2_ready=g2_ready):
                                if mc == 0:
                                    e.wait_ge(SI, slice_loaded[idx])
                                    wt(e, g2_ready)
                                wt(e, pbank_free[bA])
                                for k in range(HC):
                                    ins = e.matmul(out=pb[bA][:],
                                                   lhsT=wsl[j][:, k * T + mc * P: k * T + (mc + 1) * P],
                                                   rhs=hB[:, ts(k, T)], start=(k == 0), stop=(k == HC - 1))
                                    if k == HC - 1:
                                        t_inc(ins)
                            mm_done = C["t"]

                            @blk.vector
                            def _(e, mcg=mcg, bA=bA, mm_done=mm_done):
                                e.wait_ge(STn, mm_done)
                                if mcg == 0:
                                    e.wait_ge(SI, x1_loadedB)
                                    wt(e, out_stored)
                                v_inc(e.scalar_tensor_tensor(out=outB[:, ts(mcg, T)], in0=pb[bA][:],
                                                             scalar=s3_s[:, mcg:mcg + 1], in1=x1B[:, ts(mcg, T)],
                                                             op0=ALU.mult, op1=ALU.add))
                            pbank_free[bA] = ("v", C["v"])
                        if mr == 1:
                            out_done = C["v"]
                            hB_free = ("t", C["t"])
                            x1_usedB = C["v"]
                    slot_free[j] = ("t", C["t"])

                @blk.sync
                def _(e, i=i, out_done=out_done):
                    e.wait_ge(SV, out_done)
                    e.dma_start(out=outT[:, ts(i, T)].rearrange("(k p) t -> p k t", p=P),
                                in_=outB[:].rearrange("p (k t) -> p k t", k=KC)).then_inc(SO, 16)
                C["o"] += 16
                out_stored = ("o", C["o"])

            @blk.sync
            def _(e):
                e.wait_ge(SO, C["o"])

    return nc


_PROG_CACHE = {}

def _get_prog():
    if "nc" not in _PROG_CACHE:
        _PROG_CACHE["nc"] = build_program()
    return _PROG_CACHE["nc"]


def _consts():
    bf = ml_dtypes.bfloat16
    dsel = np.zeros((P, DK), bf)
    for p in range(P):
        dsel[p, p % DK] = 1
    bdsel = np.zeros((P, KC * NH), bf)
    gsel = np.zeros((NH, KC * P), bf)
    for c in range(KC):
        for p in range(P):
            gg = 2 * c + p // DK
            bdsel[p, c * NH + gg] = 1
            gsel[gg, c * P + p] = 1
    ones = np.ones((P, 1), bf)
    row = np.ones((1, P), bf)
    row64 = np.ones((P, P), bf)
    onesT = np.ones((1, T), bf)
    eps = np.full((P, 1), EPS, np.float32)
    return dsel, bdsel, gsel, ones, row, row64, onesT, eps


def kernel(L0_attn_x, L0_attn_W_qkv, L0_attn_b_qkv, L0_attn_W_out, L0_attn_b_out,
           L0_ffn_w1_w, L0_ffn_w1_s, L0_ffn_w2_w, L0_ffn_w2_s,
           L0_ffn_h1_w, L0_ffn_h1_s, L0_ffn_h2_w, L0_ffn_h2_s,
           L0_ffn_w3_w, L0_ffn_w3_s):
    x = np.asarray(L0_attn_x)
    B = x.shape[0]
    wq = np.asarray(L0_attn_W_qkv)
    wq3 = wq.reshape(D, NH, 3, DK)
    wqkv = np.ascontiguousarray(wq3.transpose(0, 2, 1, 3).reshape(D, 3 * D))
    bq3 = np.asarray(L0_attn_b_qkv).reshape(NH, 3, DK)
    bqkv = np.ascontiguousarray(bq3.transpose(1, 0, 2).reshape(1, 3 * D))
    bout = np.ascontiguousarray(np.asarray(L0_attn_b_out).reshape(D // P, P).T)
    w1T = np.ascontiguousarray(np.asarray(L0_ffn_w1_w).T)
    w2T = np.ascontiguousarray(np.asarray(L0_ffn_w2_w).T)
    h1T = np.ascontiguousarray(np.asarray(L0_ffn_h1_w).T)
    h2T = np.ascontiguousarray(np.asarray(L0_ffn_h2_w).T)
    w3T = np.ascontiguousarray(np.asarray(L0_ffn_w3_w).T)
    s1 = np.ascontiguousarray(np.asarray(L0_ffn_w1_s).reshape(HC, P).T)
    s2 = np.ascontiguousarray(np.asarray(L0_ffn_w2_s).reshape(HC, P).T)
    sh1 = np.ascontiguousarray(np.asarray(L0_ffn_h1_s).reshape(HC, P).T)
    sh2 = np.ascontiguousarray(np.asarray(L0_ffn_h2_s).reshape(HC, P).T)
    s3 = np.ascontiguousarray(np.asarray(L0_ffn_w3_s).reshape(D // P, P).T)
    dsel, bdsel, gsel, ones, row, row64, onesT, eps = _consts()

    shared = dict(wqkv=wqkv, bqkv=bqkv, wout=np.ascontiguousarray(np.asarray(L0_attn_W_out)),
                  bout=bout, w1T=w1T, w2T=w2T, h1T=h1T, h2T=h2T, w3T=w3T,
                  s1=s1, s2=s2, sh1=sh1, sh2=sh2, s3=s3,
                  c_dsel=dsel, c_bdsel=bdsel, c_gsel=gsel, c_ones=ones, c_row=row,
                  c_row64=row64, c_onesT=onesT, c_eps=eps)

    in_maps = []
    for c in range(B):
        m = dict(shared)
        m["xT"] = np.ascontiguousarray(x[c].T)
        in_maps.append(m)

    nc = _get_prog()
    res = run_bass_kernel_spmd(nc, in_maps, core_ids=list(range(N_CORES)))
    out = np.stack([np.asarray(r["outT"], dtype=np.float32).T for r in res.results], axis=0)
    return out


# version 03
# speedup vs baseline: 1.2911x; 1.0261x over previous
import numpy as np
import ml_dtypes
from contextlib import ExitStack

import concourse.bass as bass
import concourse.mybir as mybir
from concourse.bass_utils import run_bass_kernel_spmd

F32 = mybir.dt.float32
BF16 = mybir.dt.bfloat16
AF = mybir.ActivationFunctionType
ALU = mybir.AluOpType
ts = bass.ts

P = 128
D = 1024
DH = 4096
NH, DK = 16, 64
S = 4096
T = 512
NT = S // T
KC = D // P
HC = DH // P
N_CORES = 8
EPS = 1e-5


def build_program():
    nc = bass.Bass("TRN2", target_bir_lowering=False)

    xT = nc.dram_tensor("xT", [D, S], F32, kind="ExternalInput").ap()
    wqkv = nc.dram_tensor("wqkv", [D, 3 * D], F32, kind="ExternalInput").ap()
    bqkv = nc.dram_tensor("bqkv", [1, 3 * D], F32, kind="ExternalInput").ap()
    wout = nc.dram_tensor("wout", [D, D], F32, kind="ExternalInput").ap()
    bout = nc.dram_tensor("bout", [P, D // P], F32, kind="ExternalInput").ap()
    w1T = nc.dram_tensor("w1T", [D, DH], F32, kind="ExternalInput").ap()
    w2T = nc.dram_tensor("w2T", [D, DH], F32, kind="ExternalInput").ap()
    h1T = nc.dram_tensor("h1T", [DH, DH], F32, kind="ExternalInput").ap()
    h2T = nc.dram_tensor("h2T", [DH, DH], F32, kind="ExternalInput").ap()
    w3T = nc.dram_tensor("w3T", [DH, D], F32, kind="ExternalInput").ap()
    s1 = nc.dram_tensor("s1", [P, HC], F32, kind="ExternalInput").ap()
    s2 = nc.dram_tensor("s2", [P, HC], F32, kind="ExternalInput").ap()
    sh1 = nc.dram_tensor("sh1", [P, HC], F32, kind="ExternalInput").ap()
    sh2 = nc.dram_tensor("sh2", [P, HC], F32, kind="ExternalInput").ap()
    s3 = nc.dram_tensor("s3", [P, D // P], F32, kind="ExternalInput").ap()
    c_dsel = nc.dram_tensor("c_dsel", [P, P], BF16, kind="ExternalInput").ap()
    c_bdsel = nc.dram_tensor("c_bdsel", [P, KC * P], BF16, kind="ExternalInput").ap()
    c_gsel = nc.dram_tensor("c_gsel", [P, KC * P], BF16, kind="ExternalInput").ap()
    c_densel = nc.dram_tensor("c_densel", [P, P], BF16, kind="ExternalInput").ap()
    c_rsel = nc.dram_tensor("c_rsel", [P, P], BF16, kind="ExternalInput").ap()
    c_brow = nc.dram_tensor("c_brow", [P, P], BF16, kind="ExternalInput").ap()
    c_col0 = nc.dram_tensor("c_col0", [P, P], BF16, kind="ExternalInput").ap()
    c_onesT = nc.dram_tensor("c_onesT", [P, T], BF16, kind="ExternalInput").ap()
    c_eps = nc.dram_tensor("c_eps", [P, 1], F32, kind="ExternalInput").ap()

    outT = nc.dram_tensor("outT", [D, S], F32, kind="ExternalOutput").ap()

    x1_d = nc.dram_tensor("x1_d", [D, S], F32, kind="Internal").ap()
    x1ln_d = nc.dram_tensor("x1ln_d", [D, S], BF16, kind="Internal").ap()
    wb1_d = nc.dram_tensor("wb1_d", [D, DH], BF16, kind="Internal").ap()
    wb2_d = nc.dram_tensor("wb2_d", [D, DH], BF16, kind="Internal").ap()
    hb1_d = nc.dram_tensor("hb1_d", [DH, DH], BF16, kind="Internal").ap()
    hb2_d = nc.dram_tensor("hb2_d", [DH, DH], BF16, kind="Internal").ap()
    wb3_d = nc.dram_tensor("wb3_d", [DH, D], BF16, kind="Internal").ap()

    with ExitStack() as g:
        E = g.enter_context
        dsel_s = E(nc.sbuf_tensor("dsel_s", [P, P], BF16))
        bdsel_s = E(nc.sbuf_tensor("bdsel_s", [P, KC * P], BF16))
        gsel_s = E(nc.sbuf_tensor("gsel_s", [P, KC * P], BF16))
        densel_s = E(nc.sbuf_tensor("densel_s", [P, P], BF16))
        rsel_s = E(nc.sbuf_tensor("rsel_s", [P, P], BF16))
        brow_s = E(nc.sbuf_tensor("brow_s", [P, P], BF16))
        col0_s = E(nc.sbuf_tensor("col0_s", [P, P], BF16))
        onesT_s = E(nc.sbuf_tensor("onesT_s", [P, T], BF16))
        eps_s = E(nc.sbuf_tensor("eps_s", [P, 1], F32))
        bqkv_s = E(nc.sbuf_tensor("bqkv_s", [P, 3 * D], BF16))
        bout_s = E(nc.sbuf_tensor("bout_s", [P, D // P], F32))
        s1_s = E(nc.sbuf_tensor("s1_s", [P, HC], F32))
        s2_s = E(nc.sbuf_tensor("s2_s", [P, HC], F32))
        sh1_s = E(nc.sbuf_tensor("sh1_s", [P, HC], F32))
        sh2_s = E(nc.sbuf_tensor("sh2_s", [P, HC], F32))
        s3_s = E(nc.sbuf_tensor("s3_s", [P, D // P], F32))

        pb = [E(nc.psum_tensor(f"pb{i}", [P, T], F32)) for i in range(8)]

        SW = E(nc.semaphore("SW"))
        SI = E(nc.semaphore("SI"))
        SO = E(nc.semaphore("SO"))
        STn = E(nc.semaphore("STn"))
        SV = E(nc.semaphore("SV"))
        SS = E(nc.semaphore("SS"))
        blk = E(nc.Block())
        C = dict(w=0, i=0, o=0, t=0, v=0, s=0)
        SEMS = {"t": STn, "v": SV, "s": SS, "i": SI, "o": SO, "w": SW}

        def t_inc(instr):
            C["t"] += 1
            return instr.then_inc(STn, 1)

        def v_inc(instr):
            C["v"] += 1
            return instr.then_inc(SV, 1)

        def s_inc(instr):
            C["s"] += 1
            return instr.then_inc(SS, 1)

        def wt(e, tok):
            if tok is None:
                return
            if tok[1] > 0:
                e.wait_ge(SEMS[tok[0]], tok[1])

        @blk.sync
        def _(e):
            for dst, src in [(dsel_s, c_dsel), (bdsel_s, c_bdsel), (gsel_s, c_gsel),
                             (densel_s, c_densel), (rsel_s, c_rsel), (brow_s, c_brow),
                             (col0_s, c_col0), (onesT_s, c_onesT), (eps_s, c_eps),
                             (bout_s, bout),
                             (s1_s, s1), (s2_s, s2), (sh1_s, sh1), (sh2_s, sh2), (s3_s, s3)]:
                e.dma_start(out=dst[:], in_=src).then_inc(SI, 16)
                C["i"] += 16
        consts_loaded = C["i"]

        @blk.tensor
        def _(e):
            e.wait_ge(SI, consts_loaded)
        @blk.scalar
        def _(e):
            e.wait_ge(SI, consts_loaded)

        with ExitStack() as pa:
            EA = pa.enter_context
            wqkv_s = EA(nc.sbuf_tensor("wqkv_s", [P, KC * 3 * D], BF16))
            wout_s = EA(nc.sbuf_tensor("wout_s", [P, KC * D], BF16))
            xbuf = [EA(nc.sbuf_tensor(f"xbuf{j}", [P, KC * T], F32)) for j in range(2)]
            xbb = [EA(nc.sbuf_tensor(f"xbb{j}", [P, T], BF16)) for j in range(2)]
            sqb = [EA(nc.sbuf_tensor(f"sqb{j}", [P, T], BF16)) for j in range(2)]
            t1f = EA(nc.sbuf_tensor("t1f", [P, T], F32))
            rows = EA(nc.sbuf_tensor("rows", [1, 5 * T], F32))
            rowb = EA(nc.sbuf_tensor("rowb", [P, 2 * T], BF16))
            xln_s = EA(nc.sbuf_tensor("xln_s", [P, KC * T], BF16))
            q_s = EA(nc.sbuf_tensor("q_s", [P, KC * T], BF16))
            k_s = EA(nc.sbuf_tensor("k_s", [P, KC * T], BF16))
            v_s = EA(nc.sbuf_tensor("v_s", [P, KC * T], BF16))
            qrep = EA(nc.sbuf_tensor("qrep", [P, T], BF16))
            tmpa = [EA(nc.sbuf_tensor(f"tmpa{j}", [P, KC * T], BF16)) for j in range(2)]
            esb = [EA(nc.sbuf_tensor(f"esb{j}", [P, T], BF16)) for j in range(2)]
            tmp2 = [EA(nc.sbuf_tensor(f"tmp2_{j}", [P, T], BF16)) for j in range(2)]
            rband = EA(nc.sbuf_tensor("rband", [33, T], F32))
            rbandb = EA(nc.sbuf_tensor("rbandb", [P, 2 * T], BF16))
            ctxb = EA(nc.sbuf_tensor("ctxb", [P, KC * T], BF16))
            x1_s = EA(nc.sbuf_tensor("x1_s", [P, KC * T], F32))

            @blk.vector
            def _(e):
                e.memset(esb[0][:], 0.0)
                e.memset(esb[1][:], 0.0)
                e.memset(rbandb[:], 0.0)
                e.memset(rowb[:], 0.0)
                v_inc(e.memset(bqkv_s[:], 0.0))
            memset_done = C["v"]

            @blk.gpsimd
            def _(e):
                e.dma_start(out=wqkv_s[:].rearrange("p (k m) -> p k m", k=KC),
                            in_=wqkv.rearrange("(k p) m -> p k m", p=P)).then_inc(SW, 16)
                e.dma_start(out=wout_s[:].rearrange("p (k m) -> p k m", k=KC),
                            in_=wout.rearrange("(k p) m -> p k m", p=P)).then_inc(SW, 16)
                e.wait_ge(SV, memset_done)
                e.dma_start(out=bqkv_s[0:1, :], in_=bqkv).then_inc(SW, 16)
            C["w"] += 48
            qkvw_loaded = C["w"]

            conv = {}
            @blk.gpsimd
            def _(e):
                for name, src, dst, nrows in [("w1", w1T, wb1_d, D), ("w2", w2T, wb2_d, D),
                                              ("h1", h1T, hb1_d, DH), ("h2", h2T, hb2_d, DH),
                                              ("w3", w3T, wb3_d, DH)]:
                    step = 256
                    for r0 in range(0, nrows, step):
                        e.dma_start(out=dst[r0:r0 + step, :], in_=src[r0:r0 + step, :]).then_inc(SW, 16)
                        C["w"] += 16
                    conv[name] = C["w"]

            x_free = [None, None]
            x_loaded = [0, 0]
            x1_store = None
            x1ln_store = None
            rowb_free = None
            xln_free = None
            qkv_free = None
            ctxb_free = None
            pbank_free = [None] * 8
            tmpa_free = [0, 0]
            esb_free = [0, 0]
            psS_free = [None, None]
            psU_free = [None, None]
            rbandb_free = [0, 0]
            tmp2_free = [0, 0]
            psE_free = [0, 0]

            @blk.sync
            def _(e):
                e.dma_start(out=xbuf[0][:].rearrange("p (k t) -> p k t", k=KC),
                            in_=xT[:, 0:T].rearrange("(k p) t -> p k t", p=P)).then_inc(SI, 16)
            C["i"] += 16
            x_loaded[0] = C["i"]

            def emit_ln(src_f32, dst_bf):
                nonlocal rowb_free
                mm_tok = [0, 0]
                for c in range(KC):
                    @blk.vector
                    def _(e, c=c):
                        if mm_tok[c % 2] > 0:
                            e.wait_ge(STn, mm_tok[c % 2])
                        e.tensor_copy(out=xbb[c % 2][:], in_=src_f32[:, ts(c, T)])
                        v_inc(e.tensor_tensor(out=sqb[c % 2][:], in0=xbb[c % 2][:], in1=xbb[c % 2][:], op=ALU.mult))
                    sq_done = C["v"]

                    @blk.tensor
                    def _(e, c=c, sq_done=sq_done):
                        e.wait_ge(SV, sq_done)
                        if c == 0:
                            wt(e, pbank_free[4])
                            wt(e, pbank_free[5])
                        e.matmul(out=pb[4][:], lhsT=col0_s[:], rhs=xbb[c % 2][:],
                                 start=(c == 0), stop=(c == KC - 1), skip_group_check=True)
                        t_inc(e.matmul(out=pb[5][:], lhsT=col0_s[:], rhs=sqb[c % 2][:],
                                       start=(c == 0), stop=(c == KC - 1), skip_group_check=True))
                    mm_tok[c % 2] = C["t"]
                stats_done = C["t"]

                m_row = rows[:, 0:T]; v_row = rows[:, T:2 * T]; msq_row = rows[:, 2 * T:3 * T]
                r_row = rows[:, 3 * T:4 * T]; mr_row = rows[:, 4 * T:5 * T]

                @blk.vector
                def _(e):
                    e.wait_ge(STn, stats_done)
                    e.tensor_scalar(out=m_row, in0=pb[4][0:1, :], scalar1=1.0 / D, scalar2=None, op0=ALU.mult)
                    e.tensor_scalar(out=v_row, in0=pb[5][0:1, :], scalar1=1.0 / D, scalar2=None, op0=ALU.mult)
                    e.tensor_tensor(out=msq_row, in0=m_row, in1=m_row, op=ALU.mult)
                    v_inc(e.tensor_tensor(out=v_row, in0=v_row, in1=msq_row, op=ALU.subtract))
                var_done = C["v"]
                pbank_free[4] = ("v", var_done)
                pbank_free[5] = ("v", var_done)

                @blk.scalar
                def _(e):
                    e.wait_ge(SV, var_done)
                    e.activation(out=r_row, in_=v_row, func=AF.Ln, bias=eps_s[0:1, :])
                    s_inc(e.activation(out=r_row, in_=r_row, func=AF.Exp, scale=-0.5))
                r_done = C["s"]

                @blk.vector
                def _(e):
                    e.wait_ge(SS, r_done)
                    wt(e, rowb_free)
                    e.tensor_tensor(out=mr_row, in0=m_row, in1=r_row, op=ALU.mult)
                    e.tensor_copy(out=rowb[0:1, 0:T], in_=r_row)
                    v_inc(e.tensor_copy(out=rowb[0:1, T:2 * T], in_=mr_row))
                rowb_done = C["v"]

                @blk.tensor
                def _(e):
                    e.wait_ge(SV, rowb_done)
                    wt(e, pbank_free[2])
                    wt(e, pbank_free[3])
                    e.matmul(out=pb[2][:], lhsT=brow_s[:], rhs=rowb[:, 0:T], start=True, stop=True)
                    t_inc(e.matmul(out=pb[3][:], lhsT=brow_s[:], rhs=rowb[:, T:2 * T], start=True, stop=True))
                bcast_done = C["t"]
                rowb_free = ("t", bcast_done)

                @blk.vector
                def _(e):
                    e.wait_ge(STn, bcast_done)
                    ins = None
                    for c in range(KC):
                        e.tensor_tensor(out=t1f[:], in0=src_f32[:, ts(c, T)], in1=pb[2][:], op=ALU.mult)
                        ins = e.tensor_tensor(out=dst_bf[:, ts(c, T)], in0=t1f[:], in1=pb[3][:], op=ALU.subtract)
                    v_inc(ins)
                apply_done = C["v"]
                pbank_free[2] = ("v", apply_done)
                pbank_free[3] = ("v", apply_done)
                return apply_done

            for i in range(NT):
                xb_i = xbuf[i % 2]

                if i + 1 < NT:
                    @blk.sync
                    def _(e, i=i):
                        wt(e, x_free[(i + 1) % 2])
                        e.dma_start(out=xbuf[(i + 1) % 2][:].rearrange("p (k t) -> p k t", k=KC),
                                    in_=xT[:, ts(i + 1, T)].rearrange("(k p) t -> p k t", p=P)).then_inc(SI, 16)
                    C["i"] += 16
                    x_loaded[(i + 1) % 2] = C["i"]

                @blk.vector
                def _(e, i=i):
                    e.wait_ge(SI, x_loaded[i % 2])
                    wt(e, xln_free)
                    wt(e, x1ln_store)
                    e.engine_nop()
                ln1_done = emit_ln(xb_i, xln_s)
                psE_free[0] = max(psE_free[0], ln1_done)
                psE_free[1] = max(psE_free[1], ln1_done)

                mm_done_at = {}
                for m in range(3 * D // P):
                    bank = m % 2
                    @blk.tensor
                    def _(e, m=m, bank=bank):
                        if m == 0:
                            e.wait_ge(SV, ln1_done)
                            e.wait_ge(SW, qkvw_loaded)
                        wt(e, pbank_free[bank])
                        for k in range(KC):
                            e.matmul(out=pb[bank][:],
                                     lhsT=wqkv_s[:, k * (3 * D) + m * P: k * (3 * D) + (m + 1) * P],
                                     rhs=xln_s[:, ts(k, T)], start=(k == 0), stop=False)
                        t_inc(e.matmul(out=pb[bank][:], lhsT=bqkv_s[:, m * P:(m + 1) * P],
                                       rhs=onesT_s[:], start=False, stop=True))
                    mm_done_at[m] = C["t"]
                    pbank_free[bank] = ("s", C["s"] + m + 1)

                evac_base = C["s"]
                @blk.scalar
                def _(e):
                    wt(e, qkv_free)
                    for m in range(3 * D // P):
                        e.wait_ge(STn, mm_done_at[m])
                        dst = (q_s, k_s, v_s)[m // KC]
                        col = (m % KC) * T
                        s_inc(e.activation(out=dst[:, col:col + T], in_=pb[m % 2][:], func=AF.Copy))
                qkv_done = C["s"]
                assert qkv_done == evac_base + 3 * D // P
                xln_free = ("t", mm_done_at[3 * D // P - 1])

                for h in range(NH):
                    c_q, r_q = h // 2, (h % 2) * DK
                    sb = 6 + (h % 2)
                    ub = 4 + (h % 2)

                    @blk.vector
                    def _(e, h=h, c_q=c_q, r_q=r_q):
                        if h == 0:
                            e.wait_ge(SS, qkv_done)
                        if tmpa_free[h % 2] > 0:
                            e.wait_ge(STn, tmpa_free[h % 2])
                        e.tensor_copy(out=qrep[0:DK, :], in_=q_s[r_q:r_q + DK, ts(c_q, T)])
                        e.tensor_copy(out=qrep[DK:P, :], in_=q_s[r_q:r_q + DK, ts(c_q, T)])
                        ins = None
                        for c in range(KC):
                            ins = e.tensor_tensor(out=tmpa[h % 2][:, ts(c, T)], in0=k_s[:, ts(c, T)],
                                                  in1=qrep[:], op=ALU.mult)
                        v_inc(ins)
                    tmp_ready = C["v"]

                    @blk.tensor
                    def _(e, h=h, sb=sb, tmp_ready=tmp_ready):
                        e.wait_ge(SV, tmp_ready)
                        wt(e, psS_free[h % 2])
                        ins = None
                        for c in range(KC):
                            ins = e.matmul(out=pb[sb][:], lhsT=bdsel_s[:, ts(c, P)],
                                           rhs=tmpa[h % 2][:, ts(c, T)],
                                           start=(c == 0), stop=(c == KC - 1), skip_group_check=True)
                        t_inc(ins)
                    sc_done = C["t"]
                    tmpa_free[h % 2] = sc_done

                    exp_inc = C["s"] + 1
                    r_inc = C["s"] + 2
                    evac_inc = C["s"] + 3

                    den_done = None
                    for j in range(4):
                        cpair = (2 * j, 2 * j + 1)
                        erep_tok = {}
                        @blk.tensor
                        def _(e, h=h, j=j, cpair=cpair, sb=sb, erep_tok=erep_tok, i=i):
                            if j == 0:
                                e.wait_ge(SS, exp_inc)
                                if i == 0 and h == 0:
                                    e.wait_ge(SV, memset_done)
                            for c in cpair:
                                gp = c % 2
                                if psE_free[gp] > 0:
                                    e.wait_ge(SV, psE_free[gp])
                                t_inc(e.matmul(out=pb[2 + gp][:], lhsT=gsel_s[:, ts(c, P)],
                                               rhs=esb[h % 2][:], start=True, stop=True))
                                erep_tok[c] = C["t"]
                            if j == 0:
                                t_inc(e.matmul(out=pb[sb][:], lhsT=densel_s[:],
                                               rhs=esb[h % 2][:], start=True, stop=True))
                        if j == 0:
                            den_done = C["t"]

                        t2_tok = {}
                        @blk.vector
                        def _(e, h=h, cpair=cpair, erep_tok=erep_tok, t2_tok=t2_tok):
                            for c in cpair:
                                gp = c % 2
                                e.wait_ge(STn, erep_tok[c])
                                if tmp2_free[gp] > 0:
                                    e.wait_ge(STn, tmp2_free[gp])
                                v_inc(e.tensor_tensor(out=tmp2[gp][:], in0=pb[2 + gp][:],
                                                      in1=v_s[:, ts(c, T)], op=ALU.mult))
                                t2_tok[c] = C["v"]
                                psE_free[gp] = C["v"]

                        @blk.tensor
                        def _(e, h=h, cpair=cpair, ub=ub, t2_tok=t2_tok):
                            for c in cpair:
                                gp = c % 2
                                e.wait_ge(SV, t2_tok[c])
                                if c == 0:
                                    wt(e, psU_free[h % 2])
                                t_inc(e.matmul(out=pb[ub][:], lhsT=dsel_s[:], rhs=tmp2[gp][:],
                                               start=(c == 0), stop=(c == KC - 1), skip_group_check=True))
                                tmp2_free[gp] = C["t"]
                    u_done = C["t"]

                    @blk.tensor
                    def _(e, h=h, sb=sb, r_inc=r_inc):
                        e.wait_ge(SS, r_inc)
                        t_inc(e.matmul(out=pb[sb][:], lhsT=rsel_s[:],
                                       rhs=rbandb[:, ts(h % 2, T)], start=True, stop=True))
                    rrep_done = C["t"]

                    @blk.scalar
                    def _(e, h=h, sb=sb, ub=ub, c_q=c_q, r_q=r_q, sc_done=sc_done,
                          den_done=den_done, u_done=u_done):
                        e.wait_ge(STn, sc_done)
                        if esb_free[h % 2] > 0:
                            e.wait_ge(STn, esb_free[h % 2])
                        s_inc(e.activation(out=esb[h % 2][0:NH, :], in_=pb[sb][0:NH, :],
                                           func=AF.Exp, scale=0.125))
                        e.wait_ge(STn, den_done)
                        if rbandb_free[h % 2] > 0:
                            e.wait_ge(STn, rbandb_free[h % 2])
                        e.activation(out=rband[32:33, :], in_=pb[sb][32:33, :], func=AF.Ln)
                        s_inc(e.activation(out=rbandb[32:33, ts(h % 2, T)], in_=rband[32:33, :],
                                           func=AF.Exp, scale=-1.0))
                        e.wait_ge(STn, u_done)
                        if h < 2:
                            wt(e, ctxb_free)
                        s_inc(e.activation(out=ctxb[r_q:r_q + DK, ts(c_q, T)], in_=pb[ub][0:DK, :],
                                           func=AF.Copy))
                    assert C["s"] == evac_inc
                    esb_free[h % 2] = u_done
                    rbandb_free[h % 2] = rrep_done
                    psU_free[h % 2] = ("s", evac_inc)

                    @blk.vector
                    def _(e, h=h, sb=sb, c_q=c_q, r_q=r_q, rrep_done=rrep_done, evac_inc=evac_inc):
                        e.wait_ge(STn, rrep_done)
                        e.wait_ge(SS, evac_inc)
                        if h < 2:
                            wt(e, ctxb_free)
                        v_inc(e.tensor_tensor(out=ctxb[r_q:r_q + DK, ts(c_q, T)],
                                              in0=ctxb[r_q:r_q + DK, ts(c_q, T)],
                                              in1=pb[sb][r_q:r_q + DK, :], op=ALU.mult))
                    psS_free[h % 2] = ("v", C["v"])

                ctx_done = C["v"]
                qkv_free = ("v", ctx_done)
                pbank_free[4] = psU_free[0]
                pbank_free[5] = psU_free[1]
                pbank_free[2] = ("v", psE_free[0])
                pbank_free[3] = ("v", psE_free[1])

                for m in range(KC):
                    bank = m % 2
                    @blk.tensor
                    def _(e, m=m, bank=bank):
                        if m == 0:
                            e.wait_ge(SV, ctx_done)
                        wt(e, pbank_free[bank])
                        ins = None
                        for k in range(KC):
                            ins = e.matmul(out=pb[bank][:],
                                           lhsT=wout_s[:, k * D + m * P: k * D + (m + 1) * P],
                                           rhs=ctxb[:, ts(k, T)], start=(k == 0), stop=(k == KC - 1))
                        t_inc(ins)
                    wmm = C["t"]
                    if m == KC - 1:
                        ctxb_free = ("t", wmm)

                    @blk.vector
                    def _(e, m=m, bank=bank, wmm=wmm):
                        e.wait_ge(STn, wmm)
                        if m == 0:
                            wt(e, x1_store)
                        v_inc(e.scalar_tensor_tensor(out=x1_s[:, ts(m, T)], in0=pb[bank][:],
                                                     scalar=bout_s[:, m:m + 1], in1=xb_i[:, ts(m, T)],
                                                     op0=ALU.add, op1=ALU.add))
                    pbank_free[bank] = ("v", C["v"])
                x1_done = C["v"]
                x_free[i % 2] = ("v", x1_done)

                @blk.sync
                def _(e, i=i):
                    e.wait_ge(SV, x1_done)
                    e.dma_start(out=x1_d[:, ts(i, T)].rearrange("(k p) t -> p k t", p=P),
                                in_=x1_s[:].rearrange("p (k t) -> p k t", k=KC)).then_inc(SO, 16)
                C["o"] += 16
                x1_store = ("o", C["o"])

                @blk.vector
                def _(e):
                    wt(e, x1ln_store)
                    wt(e, xln_free)
                    e.engine_nop()
                ln2_done = emit_ln(x1_s, xln_s)

                @blk.sync
                def _(e, i=i, ln2_done=ln2_done):
                    e.wait_ge(SV, ln2_done)
                    e.dma_start(out=x1ln_d[:, ts(i, T)].rearrange("(k p) t -> p k t", p=P),
                                in_=xln_s[:].rearrange("p (k t) -> p k t", k=KC)).then_inc(SO, 16)
                C["o"] += 16
                x1ln_store = ("o", C["o"])

            phaseA_stores = C["o"]

        with ExitStack() as pbx:
            EB = pbx.enter_context
            wsl = [EB(nc.sbuf_tensor(f"wsl{j}", [P, HC * T], BF16)) for j in range(2)]
            x1lnB = [EB(nc.sbuf_tensor(f"x1lnB{j}", [P, KC * T], BF16)) for j in range(2)]
            hB = EB(nc.sbuf_tensor("hB", [P, HC * T], BF16))
            g1B = EB(nc.sbuf_tensor("g1B", [P, HC * T], BF16))
            sA = [EB(nc.sbuf_tensor(f"sA{j}", [P, T], BF16)) for j in range(2)]
            x1B = EB(nc.sbuf_tensor("x1B", [P, KC * T], F32))
            outB = EB(nc.sbuf_tensor("outB", [P, KC * T], F32))

            slot_free = [None, None]
            x1ln_loadedB = [0, 0]
            x1ln_freeB = [None, None]
            x1_loadedB = 0
            x1_usedB = 0
            out_stored = None
            out_done = 0
            hB_free = None
            g1_freeB = None
            h_ready = None
            g1_ready = None
            g2_ready = None
            hB_mm_last = 0

            @blk.sync
            def _(e):
                e.wait_ge(SO, phaseA_stores)
                e.dma_start(out=x1lnB[0][:].rearrange("p (k t) -> p k t", k=KC),
                            in_=x1ln_d[:, 0:T].rearrange("(k p) t -> p k t", p=P)).then_inc(SI, 16)
            C["i"] += 16
            x1ln_loadedB[0] = C["i"]

            sched = [("w12", mr) for mr in range(8)] + [("h1", mr) for mr in range(8)] + \
                    [("h2", mr) for mr in range(8)] + [("w3", mr) for mr in range(2)]

            for i in range(NT):
                if i + 1 < NT:
                    @blk.sync
                    def _(e, i=i):
                        wt(e, x1ln_freeB[(i + 1) % 2])
                        e.dma_start(out=x1lnB[(i + 1) % 2][:].rearrange("p (k t) -> p k t", k=KC),
                                    in_=x1ln_d[:, ts(i + 1, T)].rearrange("(k p) t -> p k t", p=P)).then_inc(SI, 16)
                    C["i"] += 16
                    x1ln_loadedB[(i + 1) % 2] = C["i"]

                slice_loaded = {}
                for idx, (kind, mr) in enumerate(sched):
                    j = idx % 2

                    if kind == "w3" and mr == 0:
                        @blk.sync
                        def _(e, i=i):
                            if x1_usedB > 0:
                                e.wait_ge(SV, x1_usedB)
                            e.dma_start(out=x1B[:].rearrange("p (k t) -> p k t", k=KC),
                                        in_=x1_d[:, ts(i, T)].rearrange("(k p) t -> p k t", p=P)).then_inc(SI, 16)
                        C["i"] += 16
                        x1_loadedB = C["i"]

                    @blk.sync
                    def _(e, kind=kind, mr=mr, j=j):
                        wt(e, slot_free[j])
                        need = {"w12": conv["w2"], "h1": conv["h1"], "h2": conv["h2"], "w3": conv["w3"]}[kind]
                        e.wait_ge(SW, need)
                        if kind == "w12":
                            e.dma_start(out=wsl[j][:, 0:KC * T].rearrange("p (k m) -> p k m", k=KC),
                                        in_=wb1_d[:, ts(mr, T)].rearrange("(k p) m -> p k m", p=P)).then_inc(SI, 16)
                            e.dma_start(out=wsl[j][:, KC * T:2 * KC * T].rearrange("p (k m) -> p k m", k=KC),
                                        in_=wb2_d[:, ts(mr, T)].rearrange("(k p) m -> p k m", p=P)).then_inc(SI, 16)
                        else:
                            src = {"h1": hb1_d, "h2": hb2_d, "w3": wb3_d}[kind]
                            e.dma_start(out=wsl[j][:].rearrange("p (k m) -> p k m", k=HC),
                                        in_=src[:, ts(mr, T)].rearrange("(k p) m -> p k m", p=P)).then_inc(SI, 16)
                    C["i"] += 32 if kind == "w12" else 16
                    slice_loaded[idx] = C["i"]

                    if kind == "w12":
                        silu_incs = {}
                        mm_toks = {}
                        for mc in range(4):
                            bA = mc % 2
                            bB = 2 + (mc % 2)
                            silu_incs[mc] = C["s"] + mc + 1
                            @blk.tensor
                            def _(e, j=j, mc=mc, bA=bA, bB=bB, idx=idx, i=i):
                                if mc == 0:
                                    e.wait_ge(SI, slice_loaded[idx])
                                    if idx == 0:
                                        e.wait_ge(SI, x1ln_loadedB[i % 2])
                                wt(e, pbank_free[bA])
                                for k in range(KC):
                                    ins = e.matmul(out=pb[bA][:],
                                                   lhsT=wsl[j][:, k * T + mc * P: k * T + (mc + 1) * P],
                                                   rhs=x1lnB[i % 2][:, ts(k, T)], start=(k == 0), stop=(k == KC - 1))
                                    if k == KC - 1:
                                        t_inc(ins)
                                wt(e, pbank_free[bB])
                                for k in range(KC):
                                    ins = e.matmul(out=pb[bB][:],
                                                   lhsT=wsl[j][:, KC * T + k * T + mc * P: KC * T + k * T + (mc + 1) * P],
                                                   rhs=x1lnB[i % 2][:, ts(k, T)], start=(k == 0), stop=(k == KC - 1))
                                    if k == KC - 1:
                                        t_inc(ins)
                            mm_toks[mc] = (C["t"] - 1, C["t"])
                            pbank_free[bA] = ("s", silu_incs[mc])

                        @blk.scalar
                        def _(e, mr=mr, mm_toks=mm_toks):
                            for mc in range(4):
                                mcg = mr * 4 + mc
                                e.wait_ge(STn, mm_toks[mc][0])
                                s_inc(e.activation(out=sA[mc % 2][:], in_=pb[mc % 2][:], func=AF.Silu,
                                                   scale=s1_s[:, mcg:mcg + 1]))
                        assert C["s"] == silu_incs[3]

                        for mc in range(4):
                            bB = 2 + (mc % 2)
                            mcg = mr * 4 + mc
                            @blk.vector
                            def _(e, mc=mc, mcg=mcg, bB=bB, mm_toks=mm_toks, silu_incs=silu_incs):
                                e.wait_ge(STn, mm_toks[mc][1])
                                e.wait_ge(SS, silu_incs[mc])
                                if mcg == 0:
                                    wt(e, hB_free)
                                v_inc(e.scalar_tensor_tensor(out=hB[:, ts(mcg, T)], in0=pb[bB][:],
                                                             scalar=s2_s[:, mcg:mcg + 1], in1=sA[mc % 2][:],
                                                             op0=ALU.mult, op1=ALU.mult))
                            pbank_free[bB] = ("v", C["v"])
                        if mr == 7:
                            h_ready = ("v", C["v"])
                            x1ln_freeB[i % 2] = ("t", C["t"])
                    elif kind in ("h1", "h2"):
                        rhs_buf = hB if kind == "h1" else g1B
                        dst_buf = g1B if kind == "h1" else hB
                        scol = sh1_s if kind == "h1" else sh2_s
                        rhs_ready = h_ready if kind == "h1" else g1_ready
                        gelu_incs = {}
                        mm_toks = {}
                        for mc in range(4):
                            bA = mc % 2
                            gelu_incs[mc] = C["s"] + mc + 1
                            @blk.tensor
                            def _(e, j=j, mc=mc, bA=bA, idx=idx, rhs_buf=rhs_buf, rhs_ready=rhs_ready):
                                if mc == 0:
                                    e.wait_ge(SI, slice_loaded[idx])
                                    wt(e, rhs_ready)
                                wt(e, pbank_free[bA])
                                for k in range(HC):
                                    ins = e.matmul(out=pb[bA][:],
                                                   lhsT=wsl[j][:, k * T + mc * P: k * T + (mc + 1) * P],
                                                   rhs=rhs_buf[:, ts(k, T)], start=(k == 0), stop=(k == HC - 1))
                                    if k == HC - 1:
                                        t_inc(ins)
                            mm_toks[mc] = C["t"]
                            pbank_free[bA] = ("s", gelu_incs[mc])

                        @blk.scalar
                        def _(e, mr=mr, kind=kind, dst_buf=dst_buf, scol=scol, mm_toks=mm_toks):
                            for mc in range(4):
                                mcg = mr * 4 + mc
                                e.wait_ge(STn, mm_toks[mc])
                                if mcg == 0:
                                    if kind == "h1":
                                        wt(e, g1_freeB)
                                    else:
                                        e.wait_ge(STn, hB_mm_last)
                                s_inc(e.activation(out=dst_buf[:, ts(mcg, T)], in_=pb[mc % 2][:], func=AF.Gelu,
                                                   scale=scol[:, mcg:mcg + 1]))
                        assert C["s"] == gelu_incs[3]
                        if kind == "h1" and mr == 7:
                            g1_ready = ("s", C["s"])
                            hB_mm_last = C["t"]
                        if kind == "h2" and mr == 7:
                            g2_ready = ("s", C["s"])
                            g1_freeB = ("t", C["t"])
                    else:
                        for mc in range(4):
                            mcg = mr * 4 + mc
                            bA = mc % 2
                            @blk.tensor
                            def _(e, j=j, mc=mc, bA=bA, idx=idx, g2_ready=g2_ready):
                                if mc == 0:
                                    e.wait_ge(SI, slice_loaded[idx])
                                    wt(e, g2_ready)
                                wt(e, pbank_free[bA])
                                for k in range(HC):
                                    ins = e.matmul(out=pb[bA][:],
                                                   lhsT=wsl[j][:, k * T + mc * P: k * T + (mc + 1) * P],
                                                   rhs=hB[:, ts(k, T)], start=(k == 0), stop=(k == HC - 1))
                                    if k == HC - 1:
                                        t_inc(ins)
                            mm_done = C["t"]

                            @blk.vector
                            def _(e, mcg=mcg, bA=bA, mm_done=mm_done):
                                e.wait_ge(STn, mm_done)
                                if mcg == 0:
                                    e.wait_ge(SI, x1_loadedB)
                                    wt(e, out_stored)
                                v_inc(e.scalar_tensor_tensor(out=outB[:, ts(mcg, T)], in0=pb[bA][:],
                                                             scalar=s3_s[:, mcg:mcg + 1], in1=x1B[:, ts(mcg, T)],
                                                             op0=ALU.mult, op1=ALU.add))
                            pbank_free[bA] = ("v", C["v"])
                        if mr == 1:
                            out_done = C["v"]
                            hB_free = ("t", C["t"])
                            x1_usedB = C["v"]
                    slot_free[j] = ("t", C["t"])

                @blk.sync
                def _(e, i=i, out_done=out_done):
                    e.wait_ge(SV, out_done)
                    e.dma_start(out=outT[:, ts(i, T)].rearrange("(k p) t -> p k t", p=P),
                                in_=outB[:].rearrange("p (k t) -> p k t", k=KC)).then_inc(SO, 16)
                C["o"] += 16
                out_stored = ("o", C["o"])

            @blk.sync
            def _(e):
                e.wait_ge(SO, C["o"])

    return nc


_PROG_CACHE = {}

def _get_prog():
    if "nc" not in _PROG_CACHE:
        _PROG_CACHE["nc"] = build_program()
    return _PROG_CACHE["nc"]


def _consts():
    bf = ml_dtypes.bfloat16
    dsel = np.zeros((P, P), bf)
    for p in range(P):
        dsel[p, p % DK] = 1
    bdsel = np.zeros((P, KC * P), bf)
    gsel = np.zeros((P, KC * P), bf)
    for c in range(KC):
        for p in range(P):
            gg = 2 * c + p // DK
            bdsel[p, c * P + gg] = 1
            gsel[gg, c * P + p] = 1
    densel = np.zeros((P, P), bf); densel[0:NH, 32] = 1
    rsel = np.zeros((P, P), bf); rsel[32, :] = 1
    brow = np.zeros((P, P), bf); brow[0, :] = 1
    col0 = np.zeros((P, P), bf); col0[:, 0] = 1
    onesT = np.zeros((P, T), bf); onesT[0, :] = 1
    eps = np.full((P, 1), EPS, np.float32)
    return dsel, bdsel, gsel, densel, rsel, brow, col0, onesT, eps


def kernel(L0_attn_x, L0_attn_W_qkv, L0_attn_b_qkv, L0_attn_W_out, L0_attn_b_out,
           L0_ffn_w1_w, L0_ffn_w1_s, L0_ffn_w2_w, L0_ffn_w2_s,
           L0_ffn_h1_w, L0_ffn_h1_s, L0_ffn_h2_w, L0_ffn_h2_s,
           L0_ffn_w3_w, L0_ffn_w3_s):
    x = np.asarray(L0_attn_x)
    B = x.shape[0]
    wq = np.asarray(L0_attn_W_qkv)
    wq3 = wq.reshape(D, NH, 3, DK)
    wqkv = np.ascontiguousarray(wq3.transpose(0, 2, 1, 3).reshape(D, 3 * D))
    bq3 = np.asarray(L0_attn_b_qkv).reshape(NH, 3, DK)
    bqkv = np.ascontiguousarray(bq3.transpose(1, 0, 2).reshape(1, 3 * D))
    bout = np.ascontiguousarray(np.asarray(L0_attn_b_out).reshape(D // P, P).T)
    w1T = np.ascontiguousarray(np.asarray(L0_ffn_w1_w).T)
    w2T = np.ascontiguousarray(np.asarray(L0_ffn_w2_w).T)
    h1T = np.ascontiguousarray(np.asarray(L0_ffn_h1_w).T)
    h2T = np.ascontiguousarray(np.asarray(L0_ffn_h2_w).T)
    w3T = np.ascontiguousarray(np.asarray(L0_ffn_w3_w).T)
    s1 = np.ascontiguousarray(np.asarray(L0_ffn_w1_s).reshape(HC, P).T)
    s2 = np.ascontiguousarray(np.asarray(L0_ffn_w2_s).reshape(HC, P).T)
    sh1 = np.ascontiguousarray(np.asarray(L0_ffn_h1_s).reshape(HC, P).T)
    sh2 = np.ascontiguousarray(np.asarray(L0_ffn_h2_s).reshape(HC, P).T)
    s3 = np.ascontiguousarray(np.asarray(L0_ffn_w3_s).reshape(D // P, P).T)
    dsel, bdsel, gsel, densel, rsel, brow, col0, onesT, eps = _consts()

    shared = dict(wqkv=wqkv, bqkv=bqkv, wout=np.ascontiguousarray(np.asarray(L0_attn_W_out)),
                  bout=bout, w1T=w1T, w2T=w2T, h1T=h1T, h2T=h2T, w3T=w3T,
                  s1=s1, s2=s2, sh1=sh1, sh2=sh2, s3=s3,
                  c_dsel=dsel, c_bdsel=bdsel, c_gsel=gsel, c_densel=densel, c_rsel=rsel,
                  c_brow=brow, c_col0=col0, c_onesT=onesT, c_eps=eps)

    in_maps = []
    for c in range(B):
        m = dict(shared)
        m["xT"] = np.ascontiguousarray(x[c].T)
        in_maps.append(m)

    nc = _get_prog()
    res = run_bass_kernel_spmd(nc, in_maps, core_ids=list(range(N_CORES)))
    out = np.stack([np.asarray(r["outT"], dtype=np.float32).T for r in res.results], axis=0)
    return out
